# version 04
# baseline (speedup 1.0000x reference)
import math
import os
import numpy as np
import ml_dtypes
import concourse.bass as bass
import concourse.mybir as mybir
from concourse.bass_utils import run_bass_kernel_spmd
from contextlib import ExitStack

F32 = mybir.dt.float32
BF16 = mybir.dt.bfloat16
AF = mybir.ActivationFunctionType
ALU = mybir.AluOpType
AX = mybir.AxisListType
BF = ml_dtypes.bfloat16

D = 2048
TOK = 1024
S = 2048
NH = 16
NE = 8
HID = 1024
NK = 16
NJ = 8
NDO = 16
LAMB_INIT = 0.8 - 0.6 * math.exp(-0.3 * 12)
EPS = 1e-5
LN8 = math.log(8.0)
LN_OMLI = math.log(1.0 - LAMB_INIT)


class Ch:
    def __init__(self, h, step=1):
        self.h = h
        self.n = 0
        self.step = step


class Prog:
    def __init__(self):
        self.q = {e: [] for e in "tvagy"}
        self.waits = []

    def _emit(self, eng, fn, inc, waits):
        for (c, n) in waits:
            if n > 0:
                self.waits.append((c, n))
        step_amt = None
        if inc is not None:
            inc.n += inc.step
            step_amt = inc.step

        def run(e, fn=fn, inc=inc, waits=tuple(waits), step_amt=step_amt):
            for (c, n) in waits:
                if n > 0:
                    e.wait_ge(c.h, n)
            ins = fn(e)
            if inc is not None:
                ins.then_inc(inc.h, step_amt)
        self.q[eng].append(run)

    def t(self, fn, inc=None, waits=()):
        self._emit("t", fn, inc, waits)

    def v(self, fn, inc=None, waits=()):
        self._emit("v", fn, inc, waits)

    def a(self, fn, inc=None, waits=()):
        self._emit("a", fn, inc, waits)

    def g(self, fn, inc=None, waits=()):
        self._emit("g", fn, inc, waits)

    def y(self, fn, inc=None, waits=()):
        self._emit("y", fn, inc, waits)

    def wait(self, eng, ch, n=None):
        n = ch.n if n is None else n
        if n <= 0:
            return
        self.waits.append((ch, n))

        def run(e, ch=ch, n=n):
            e.wait_ge(ch.h, n)
        self.q[eng].append(run)

    def check(self):
        for (c, n) in self.waits:
            assert n <= c.n, f"wait {n} > final {c.n}"


def build_program():
    nc = bass.Bass("TRN2", target_bir_lowering=False)

    def din(name, shape, dt=F32):
        return nc.dram_tensor(name, shape, dt, kind="ExternalInput").ap()

    x_own = din("x_own", [TOK, D])
    x_peer = din("x_peer", [TOK, D])
    ropeC = din("ropeC", [128, S])
    ropeS2 = din("ropeS2", [128, S])
    wkH = din("wkH", [NH, 128, NK * 128])
    wqH = din("wqH", [NH, 128, NK * 128])
    wvH = din("wvH", [4, 128, NK * 512])
    woH = din("woH", [NDO, 128, NH * 128])
    gateH = din("gateH", [128, 128])
    w1H = din("w1H", [NE, NJ, 128, NK * 128])
    w3H = din("w3H", [NE, NJ, 128, NK * 128])
    w2H = din("w2H", [NE, NDO, 128, NJ * 128])
    sw1H = din("sw1H", [NJ, 128, NK * 128])
    sw3H = din("sw3H", [NJ, 128, NK * 128])
    sw2H = din("sw2H", [NDO, 128, NJ * 128])
    gamma = din("gamma", [128])
    ebias = din("ebias", [8])
    lq1 = din("lq1", [64])
    lk1 = din("lk1", [64])
    lq2 = din("lq2", [64])
    lk2 = din("lk2", [64])
    identf_d = din("identf", [128, 128])
    identb_d = din("identb", [128, 128], BF16)
    permb_d = din("permb", [128, 128], BF16)
    iota_d = din("iota1000", [128, 8])

    out_d = nc.dram_tensor("out", [TOK, D], F32, kind="ExternalOutput").ap()
    xspill = nc.dram_tensor("xspill", [128, NK * TOK], F32, kind="Internal").ap()
    dscr = nc.dram_tensor("dscr", [8192], F32, kind="Internal").ap()
    zscd = nc.dram_tensor("zscd", [8, TOK], F32, kind="Internal").ap()
    dbgd = nc.dram_tensor("dbgd", [512, 2048], F32, kind="Internal").ap()
    dbg2 = nc.dram_tensor("dbg2", [128, NK * TOK], F32, kind="Internal").ap()

    def drow(c0, c1):
        return dscr[c0:c1].rearrange("(o j) -> o j", o=1)

    with ExitStack() as ctx:
        E = ctx.enter_context
        bigA = E(nc.sbuf_tensor("bigA", [128, NK * TOK], F32))
        ln2x = E(nc.sbuf_tensor("ln2x", [128, NK * S], BF16))
        mid = E(nc.sbuf_tensor("mid", [128, NK * TOK], BF16))
        wa = E(nc.sbuf_tensor("wa", [128, 8192], BF16))
        hsb = E(nc.sbuf_tensor("hsb", [128, NJ * TOK], BF16))
        sma = E(nc.sbuf_tensor("sma", [128, 2048], F32))
        identf = E(nc.sbuf_tensor("identf_s", [128, 128], F32))
        identb = E(nc.sbuf_tensor("identb_s", [128, 128], BF16))
        permb = E(nc.sbuf_tensor("permb_s", [128, 128], BF16))
        gamb = E(nc.sbuf_tensor("gamb", [128, 128], F32))
        iot = E(nc.sbuf_tensor("iot", [128, 8], F32))
        ebb = E(nc.sbuf_tensor("ebb", [128, 8], F32))
        onesb = E(nc.sbuf_tensor("onesb", [128, 1], BF16))
        gateS = E(nc.sbuf_tensor("gateS", [128, 128], BF16))
        lamc = E(nc.sbuf_tensor("lamc", [128, 1], F32))
        nBc = E(nc.sbuf_tensor("nBc", [128, 2], F32))
        zq = E(nc.sbuf_tensor("zq", [128, 4], F32))
        att = E(nc.sbuf_tensor("att", [128, 384], F32))
        cbias = E(nc.sbuf_tensor("cbias", [128, 4], F32))
        onesr = E(nc.sbuf_tensor("onesr", [1, 128], BF16))
        lhl = E(nc.sbuf_tensor("lhl", [1, 4], BF16))

        pA = E(nc.psum_tensor("pA", [128, 1024], F32))
        pA2 = E(nc.psum_tensor("pA2", [128, 1024], F32))
        pB = E(nc.psum_tensor("pB", [128, 1024], F32))
        pC = E(nc.psum_tensor("pC", [128, 1024], F32))

        sem_names = ["LW", "LY", "MM", "VE", "AC", "GP",
                     "s0", "s1", "s2", "s3", "s4", "s5", "s6", "s7", "s8", "s9"]
        hs = {n: E(nc.semaphore(n)) for n in sem_names}
        LW = Ch(hs["LW"], 16)
        LY = Ch(hs["LY"], 16)
        MM = Ch(hs["MM"], 1)
        VE = Ch(hs["VE"], 1)
        AC = Ch(hs["AC"], 1)
        GP = Ch(hs["GP"], 1)
        SL = [Ch(hs[f"s{i}"], 16) for i in range(10)]

        p = Prog()

        def dmaW(ch, out_ap, in_ap, waits=()):
            ch.n += 16
            n = ch.n

            def run(g, o=out_ap, i=in_ap, waits=tuple(waits), ch=ch):
                for (c, w) in waits:
                    if w > 0:
                        g.wait_ge(c.h, w)
                g.dma_start(o, i).then_inc(ch.h, 16)
            for (c, w) in waits:
                if w > 0:
                    p.waits.append((c, w))
            p.q["g"].append(run)

        def dmaY(out_ap, in_ap, waits=(), ch=None):
            ch = LY if ch is None else ch
            ch.n += 16
            n = ch.n

            def run(y, o=out_ap, i=in_ap, waits=tuple(waits), ch=ch):
                for (c, w) in waits:
                    if w > 0:
                        y.wait_ge(c.h, w)
                y.dma_start(o, i).then_inc(ch.h, 16)
            for (c, w) in waits:
                if w > 0:
                    p.waits.append((c, w))
            p.q["y"].append(run)

        def xr(k, c0=0, c1=TOK):
            return bigA[:, k * TOK + c0: k * TOK + c1]

        def lnx(k):
            return mid[:, k * TOK:(k + 1) * TOK]

        def l2x(k, c0=0, c1=S):
            return ln2x[:, k * S + c0: k * S + c1]

        PH = int(os.environ.get("ATH_PH", "9"))
        STARTUP = PH >= -1
        dmaY(identf[:, :], identf_d[:, :])
        dmaY(identb[:, :], identb_d[:, :])
        dmaY(permb[:, :], permb_d[:, :])
        dmaY(iot[:, :], iota_d[:, :])
        if STARTUP:
            dmaY(gamb[:, :], gamma.rearrange("(o j) -> o j", o=1).broadcast_to([128, 128]))
            dmaY(ebb[:, :], ebias.rearrange("(o j) -> o j", o=1).broadcast_to([128, 8]))
            dmaY(sma[0:1, 0:64], lq1.rearrange("(o j) -> o j", o=1))
            dmaY(sma[0:1, 64:128], lk1.rearrange("(o j) -> o j", o=1))
            dmaY(sma[0:1, 128:192], lq2.rearrange("(o j) -> o j", o=1))
            dmaY(sma[0:1, 192:256], lk2.rearrange("(o j) -> o j", o=1))
        dmaW(LW, gateS[:, :], gateH[:, :])
        gate_lw = LW.n
        p.v(lambda v: v.memset(onesb[:, :], 1.0), inc=VE)
        p.v(lambda v: v.memset(onesr[:, :], 1.0), inc=VE)
        p.v(lambda v: v.memset(cbias[:, 0:1], EPS), inc=VE)
        p.v(lambda v: v.memset(cbias[:, 1:2], -LN8), inc=VE)
        p.v(lambda v: v.memset(cbias[:, 2:3], LN_OMLI), inc=VE)

        consts_ly = LY.n
        if not STARTUP:
            p.v(lambda v: v.memset(sma[0:1, 0:400], 0.5), inc=VE, waits=[(LY, consts_ly)])
        if STARTUP:
            p.v(lambda v: v.tensor_mul(sma[0:1, 256:320], sma[0:1, 0:64], sma[0:1, 64:128]),
                inc=VE, waits=[(LY, consts_ly)])
            p.v(lambda v: v.tensor_mul(sma[0:1, 320:384], sma[0:1, 128:192], sma[0:1, 192:256]),
                inc=VE)
            p.wait("v", VE)
            p.v(lambda v: v.reduce_sum(sma[0:1, 384:385], sma[0:1, 256:320], AX.X),
                inc=VE)
            p.v(lambda v: v.reduce_sum(sma[0:1, 385:386], sma[0:1, 320:384], AX.X), inc=VE)
            p.a(lambda a: a.activation(sma[0:1, 386:387], sma[0:1, 384:385], AF.Exp),
                waits=[(VE, VE.n)])
            p.a(lambda a: a.activation(sma[0:1, 387:388], sma[0:1, 385:386], AF.Exp), inc=AC)
            p.v(lambda v: v.tensor_sub(sma[0:1, 388:389], sma[0:1, 387:388], sma[0:1, 386:387]),
                inc=VE, waits=[(AC, AC.n)])
            p.wait("v", VE)
            p.v(lambda v: v.tensor_scalar_add(sma[0:1, 389:390], sma[0:1, 388:389], -LAMB_INIT),
                inc=VE)
            p.a(lambda a: a.copy(lhl[0:1, 0:1], sma[0:1, 389:390]),
                inc=AC, waits=[(VE, VE.n)])
            p.v(lambda v: v.tensor_sub(sma[0:1, 390:391], sma[0:1, 389:390],
                                       lhl[0:1, 0:1]),
                inc=VE, waits=[(AC, AC.n)])
            p.a(lambda a: a.copy(lhl[0:1, 1:2], sma[0:1, 390:391]),
                inc=AC, waits=[(VE, VE.n)])
            p.t(lambda t: t.matmul(pB[:, 0:1], onesr[0:1, :], lhl[0:1, 0:1],
                                   start=True, stop=False),
                waits=[(AC, AC.n), (VE, VE.n)])
            p.t(lambda t: t.matmul(pB[:, 0:1], onesr[0:1, :], lhl[0:1, 1:2],
                                   start=False, stop=True), inc=MM)
            p.a(lambda a: a.copy(lamc[:, :], pB[:, 0:1]),
                inc=AC, waits=[(MM, MM.n)])

        def emit_ln(dst_fn):
            Tn = TOK
            hb = hsb

            def xb_slot(par):
                return hb[:, par * Tn:(par + 1) * Tn]

            def xsq_slot(par):
                return hb[:, (2 + par) * Tn:(3 + par) * Tn]
            mm0 = MM.n
            ve_in0 = VE.n
            guard_mm = MM.n
            for k in range(NK):
                par = k % 2
                w = []
                if k < 2:
                    w = [(VE, ve_in0), (MM, guard_mm)]
                else:
                    w = [(MM, mm0 + 4 * (k - 1))]
                p.a(lambda a, k=k, par=par: a.copy(xb_slot(par), xr(k)),
                    inc=AC, waits=w)
                p.a(lambda a, k=k, par=par:
                    a.activation(xsq_slot(par), xr(k), AF.Square), inc=AC)
                for tc in range(2):
                    c0, c1 = tc * 512, (tc + 1) * 512
                    wm = [(AC, AC.n)]
                    if k == 0 and tc == 0:
                        wm.append((VE, VE.n))
                        wm.append((AC, AC.n))
                    p.t(lambda t, par=par, c0=c0, c1=c1, k=k:
                        t.matmul(pB[0:1, c0:c1], onesb[:, :], xb_slot(par)[:, c0:c1],
                                 start=(k == 0), stop=(k == NK - 1)),
                        inc=MM, waits=wm)
                    p.t(lambda t, par=par, c0=c0, c1=c1, k=k:
                        t.matmul(pA[0:1, c0:c1], onesb[:, :], xsq_slot(par)[:, c0:c1],
                                 start=(k == 0), stop=(k == NK - 1)),
                        inc=MM)
            mrow = sma[0:1, 1024:1024 + Tn]
            vrow = sma[0:1, 0:Tn]
            hb16 = hsb.ap()
            m_bf = hb16[0:1, 4096:4096 + Tn]
            rhi = hb16[0:1, 5120:5120 + Tn]
            rlo = hb16[0:1, 6144:6144 + Tn]
            p.a(lambda a: a.activation(mrow, pB[0:1, 0:Tn], AF.Copy, scale=1.0 / D),
                inc=AC, waits=[(MM, MM.n)])
            p.a(lambda a: a.copy(m_bf, mrow), inc=AC)
            p.v(lambda v: v.tensor_mul(vrow, mrow, mrow), waits=[(AC, AC.n)])
            p.v(lambda v: v.scalar_tensor_tensor(vrow, pA[0:1, 0:Tn], 1.0 / D,
                                                 vrow, ALU.mult, ALU.subtract), inc=VE)
            p.a(lambda a: a.activation(mrow, vrow, AF.Ln, bias=cbias[0:1, 0:1]),
                waits=[(VE, VE.n)])
            p.a(lambda a: a.activation(mrow, mrow, AF.Exp, scale=-0.5), inc=AC)
            p.a(lambda a: a.copy(rhi, mrow), inc=AC)
            p.v(lambda v: v.tensor_sub(vrow, mrow, rhi), inc=VE, waits=[(AC, AC.n)])
            p.a(lambda a: a.copy(rlo, vrow), inc=AC, waits=[(VE, VE.n)])
            mm_bc0 = MM.n
            for cc in range(2):
                p.t(lambda t, cc=cc:
                    t.matmul(pB[:, cc * 512:(cc + 1) * 512], onesr[0:1, :],
                             m_bf[0:1, cc * 512:(cc + 1) * 512],
                             start=True, stop=True),
                    inc=MM, waits=[(AC, AC.n), (VE, VE.n)] if cc == 0 else [])
                p.t(lambda t, cc=cc:
                    t.matmul(pA[:, cc * 512:(cc + 1) * 512], onesr[0:1, :],
                             rhi[0:1, cc * 512:(cc + 1) * 512],
                             start=True, stop=False),
                    inc=MM)
                p.t(lambda t, cc=cc:
                    t.matmul(pA[:, cc * 512:(cc + 1) * 512], onesr[0:1, :],
                             rlo[0:1, cc * 512:(cc + 1) * 512],
                             start=False, stop=True),
                    inc=MM)
            mm_bc = MM.n
            for k in range(NK):
                w = [(MM, mm_bc)] if k == 0 else []
                p.v(lambda v, k=k: v.tensor_sub(dst_fn(k), xr(k), pB[:, 0:Tn]),
                    inc=VE, waits=w)
            for k in range(NK):
                p.v(lambda v, k=k: v.tensor_mul(dst_fn(k), dst_fn(k), pA[:, 0:Tn]),
                    inc=VE)

        def emit_load_x(src_dram):
            midf = mid.ap().bitcast(F32)

            def xslot(par):
                return midf[:, par * 2048:(par + 1) * 2048]
            ly0 = LY.n
            mm0 = MM.n
            ve0 = VE.n
            guard = MM.n
            for tt in range(8):
                par = tt % 2
                w = [(MM, mm0 + 16 * (tt - 1))] if tt >= 2 else [(MM, guard)]
                dmaY(xslot(par), src_dram[tt * 128:(tt + 1) * 128, :], waits=w)
                for k in range(NK):
                    pk = k % 2
                    i = 16 * tt + k
                    wv = []
                    if k == 0:
                        wv.append((LY, ly0 + 16 * (tt + 1)))
                    if i >= 2:
                        wv.append((VE, ve0 + i - 1))
                    p.t(lambda t, tt=tt, k=k, pk=pk, par=par:
                        t.transpose(pC[:, pk * 512:pk * 512 + 128],
                                    xslot(par)[:, k * 128:(k + 1) * 128], identf[:, :]),
                        inc=MM, waits=wv)
                    p.v(lambda v, tt=tt, k=k, pk=pk:
                        v.tensor_copy(xr(k, tt * 128, (tt + 1) * 128),
                                      pC[:, pk * 512:pk * 512 + 128]),
                        inc=VE, waits=[(MM, MM.n)])

        ffn_dbg = [True]
        spill_ve8 = [0]

        def emit_ffn(lnsrc):
            Tn = TOK
            ntc = 2
            guard_mm = MM.n
            for k in range(NK):
                for tc in range(ntc):
                    w = []
                    if k == 0 and tc == 0:
                        w = [(LW, gate_lw), (VE, VE.n), (AC, AC.n)]
                    p.t(lambda t, k=k, tc=tc:
                        t.matmul(pB[0:8, tc * 512:(tc + 1) * 512],
                                 gateS[:, k * 8:(k + 1) * 8],
                                 lnsrc(k)[:, tc * 512:(tc + 1) * 512],
                                 start=(k == 0), stop=(k == NK - 1)),
                        inc=MM, waits=w)
            p.a(lambda a: a.activation(sma[0:8, 0:Tn], pB[0:8, 0:Tn], AF.Sigmoid),
                inc=AC, waits=[(MM, MM.n)])
            if os.environ.get("ATH_DBG") == "gate" and ffn_dbg[0]:
                dmaY(dbgd[0:8, 0:Tn], sma[0:8, 0:Tn], waits=[(AC, AC.n)])
            zfm = hsb.ap().bitcast(F32)
            bt = sma[:, 1024:1032]
            mx = sma[:, 1032:1040]
            ug = sma[:, 1040:1048]
            sg = sma[:, 1048:1056]
            t1g = sma[:, 1056:1064]
            scl = sma[:, 1064:1072]
            selg = sma[:, 1072:1080]
            zsg = sma[:, 1080:1088]
            for tt in range(Tn // 128):
                w = [(AC, AC.n)]
                if tt > 0:
                    w.append((VE, VE.n))
                p.t(lambda t, tt=tt:
                    t.transpose(pA[:, 0:8],
                                sma[0:8, tt * 128:(tt + 1) * 128], identf[0:8, 0:8]),
                    inc=MM, waits=w)
                mmt = MM.n
                p.v(lambda v: v.tensor_add(bt, pA[:, 0:8], ebb[:, :]),
                    inc=VE, waits=[(MM, mmt), (GP, GP.n)])
                p.wait("v", VE)
                p.v(lambda v: v.max(mx, bt), inc=VE)
                p.g(lambda g: g.tensor_single_scalar(ug, bt, mx[:, 1:2], ALU.subtract),
                    inc=GP, waits=[(VE, VE.n)])
                p.a(lambda a: a.activation(sg, ug, AF.Sign),
                    inc=AC, waits=[(GP, GP.n)])
                p.v(lambda v: v.tensor_scalar_add(t1g, sg, 1.0),
                    waits=[(AC, AC.n)])
                p.v(lambda v: v.tensor_scalar_max(scl, sg, 0.0), inc=VE)
                p.wait("v", VE)
                p.v(lambda v: v.tensor_sub(selg, t1g, scl), inc=VE)
                p.wait("v", VE)
                p.v(lambda v: v.tensor_mul(zsg, selg, pA[:, 0:8]), inc=VE)
                p.t(lambda t, tt=tt:
                    t.transpose(pA2[0:8, 0:128], zsg, identf[:, :]),
                    inc=MM, waits=[(VE, VE.n)])
                p.a(lambda a, tt=tt: a.copy(zfm[0:8, tt * 128:(tt + 1) * 128],
                                            pA2[0:8, 0:128]),
                    inc=AC, waits=[(MM, MM.n)])
            dmaY(zscd[:, 0:Tn], zfm[0:8, 0:Tn], waits=[(AC, AC.n)])
            zout_ly = LY.n

            def w13s(i):
                return wa[:, (i % 3) * 2048:((i % 3) + 1) * 2048]

            def w2s(i):
                return wa[:, 6144 + (i % 2) * 1024: 6144 + ((i % 2) + 1) * 1024]
            zc = sma[:, 0:Tn]
            ubuf = sma[:, 1024:1536]
            vbuf = sma[:, 1536:2048]
            w13_ch = [SL[1], SL[2], SL[3]]
            w2_ch = [SL[4], SL[5]]
            zc_ch = SL[6]
            gidx = [0]
            w2idx = [0]
            slot_mm = {}
            zc_ve_prev = [VE.n]
            first_h = [True]
            slot_ve = {0: VE.n, 1: VE.n}
            epi_ve = {}

            for e in range(NE + 1):
                shared = (e == NE)
                if not shared:
                    dmaY(zc, zscd[e:e + 1, 0:Tn].broadcast_to([128, Tn]),
                         waits=[(LY, zout_ly), (VE, zc_ve_prev[0])], ch=zc_ch)
                zc_ld = zc_ch.n
                for j in range(NJ):
                    i1s = gidx[0]
                    i3s = gidx[0] + 1
                    gidx[0] += 2
                    src1 = sw1H[j] if shared else w1H[e, j]
                    src3 = sw3H[j] if shared else w3H[e, j]
                    ch1 = w13_ch[i1s % 3]
                    ch3 = w13_ch[i3s % 3]
                    dmaW(ch1, w13s(i1s)[:, :], src1,
                         waits=[(MM, slot_mm.get(i1s % 3, guard_mm))])
                    n1 = ch1.n
                    dmaW(ch3, w13s(i3s)[:, :], src3,
                         waits=[(MM, slot_mm.get(i3s % 3, guard_mm))])
                    n3 = ch3.n
                    for tc in range(ntc):
                        for k in range(NK):
                            wts = []
                            if k == 0:
                                wts = [(ch1, n1), (VE, slot_ve[tc])]
                            p.t(lambda t, tc=tc, k=k, i1s=i1s:
                                t.matmul(pA[:, tc * 512:(tc + 1) * 512],
                                         w13s(i1s)[:, k * 128:(k + 1) * 128],
                                         lnsrc(k)[:, tc * 512:(tc + 1) * 512],
                                         start=(k == 0), stop=(k == NK - 1)),
                                inc=MM, waits=wts)
                        for k in range(NK):
                            wts = [(ch3, n3)] if k == 0 else []
                            p.t(lambda t, tc=tc, k=k, i3s=i3s:
                                t.matmul(pA2[:, tc * 512:(tc + 1) * 512],
                                         w13s(i3s)[:, k * 128:(k + 1) * 128],
                                         lnsrc(k)[:, tc * 512:(tc + 1) * 512],
                                         start=(k == 0), stop=(k == NK - 1)),
                                inc=MM, waits=wts)
                        mm_a = MM.n
                        hslot = hsb[:, j * TOK + tc * 512: j * TOK + (tc + 1) * 512]
                        if shared:
                            wts = [(MM, mm_a)]
                            p.a(lambda a, hslot=hslot, tc=tc:
                                a.activation(hslot, pA[:, tc * 512:(tc + 1) * 512],
                                             AF.Silu),
                                inc=AC, waits=wts)
                            p.v(lambda v, hslot=hslot, tc=tc:
                                v.tensor_mul(hslot, hslot,
                                             pA2[:, tc * 512:(tc + 1) * 512]),
                                inc=VE, waits=[(AC, AC.n)])
                            slot_ve[tc] = VE.n
                        else:
                            p.v(lambda v, tc=tc:
                                v.tensor_mul(vbuf, pA2[:, tc * 512:(tc + 1) * 512],
                                             zc[:, tc * 512:(tc + 1) * 512]),
                                waits=[(MM, mm_a), (zc_ch, zc_ld), (AC, AC.n)])
                            p.v(lambda v, tc=tc:
                                v.tensor_mul(ubuf, pA[:, tc * 512:(tc + 1) * 512],
                                             zc[:, tc * 512:(tc + 1) * 512]),
                                inc=VE)
                            slot_ve[tc] = VE.n
                            wts = [(VE, VE.n)]
                            if first_h[0]:
                                wts.append((LY, zout_ly))
                                first_h[0] = False
                            p.a(lambda a, hslot=hslot:
                                a.activation(hslot, ubuf, AF.Silu),
                                inc=AC, waits=wts)
                            p.v(lambda v, hslot=hslot:
                                v.tensor_mul(hslot, hslot, vbuf),
                                inc=VE, waits=[(AC, AC.n)])
                    slot_mm[i1s % 3] = MM.n
                    slot_mm[i3s % 3] = MM.n
                zc_ve_prev[0] = VE.n
                h_ve = VE.n
                for do in range(NDO):
                    iw2 = w2idx[0]
                    w2idx[0] += 1
                    ch2 = w2_ch[iw2 % 2]
                    src2 = sw2H[do] if shared else w2H[e, do]
                    dmaW(ch2, w2s(iw2)[:, :], src2,
                         waits=[(MM, slot_mm.get(("w2", iw2 % 2), guard_mm))])
                    n2 = ch2.n
                    for tc in range(ntc):
                        pbank = pB if (do % 2 == 0) else pC
                        for j in range(NJ):
                            wts = []
                            if j == 0:
                                wts = [(ch2, n2),
                                       (VE, max(epi_ve.get((do % 2, tc), 0), h_ve))]
                                if do == 0 and tc == 0:
                                    wts.append((AC, AC.n))
                            p.t(lambda t, do=do, tc=tc, j=j, iw2=iw2, pbank=pbank:
                                t.matmul(pbank[:, tc * 512:(tc + 1) * 512],
                                         w2s(iw2)[:, j * 128:(j + 1) * 128],
                                         hsb[:, j * TOK + tc * 512:
                                             j * TOK + (tc + 1) * 512],
                                         start=(j == 0), stop=(j == NJ - 1)),
                                inc=MM, waits=wts)
                        p.v(lambda v, do=do, tc=tc, pbank=pbank:
                            v.scalar_tensor_tensor(
                                xr(do, tc * 512, (tc + 1) * 512),
                                pbank[:, tc * 512:(tc + 1) * 512], 0.5,
                                xr(do, tc * 512, (tc + 1) * 512),
                                ALU.mult, ALU.add),
                            inc=VE, waits=[(MM, MM.n)])
                        epi_ve[(do % 2, tc)] = VE.n
                        if do == 7:
                            spill_ve8[0] = VE.n
                    slot_mm[("w2", iw2 % 2)] = MM.n
            ffn_dbg[0] = False

        if PH >= -2:
            emit_load_x(x_peer)
        if PH >= 1:
            emit_ln(lambda k: lnx(k))
        if PH >= 2:
            emit_ffn(lambda k: lnx(k))
        if PH >= 3:
            emit_ln(lambda k: l2x(k, 0, TOK))

        if PH >= 4:
            emit_load_x(x_own)
            emit_ln(lambda k: lnx(k))
            emit_ffn(lambda k: lnx(k))
            emit_ln(lambda k: l2x(k, TOK, S))
        if PH >= 0:
            dmaY(xspill[:, 0:8 * TOK], bigA[:, 0:8 * TOK], waits=[(VE, spill_ve8[0])])
            dmaY(xspill[:, 8 * TOK:], bigA[:, 8 * TOK:], waits=[(VE, VE.n)])
        spill_ly = LY.n

        if 0 <= PH < 5:
            dmaY(bigA[:, :], xspill[:, :], waits=[(LY, spill_ly)])
        _full = PH >= 5
        BA = bigA.ap().bitcast(BF16)
        VOFF = 0
        PTOFF = 8448
        KOFF = 16640
        QOFF = 20736
        O1OFF = 21760
        RTOFF = 22784
        TBOFF = 23808
        QSOFF = 24320
        CCOFF = 27392
        CSOFF = 29440

        def Vslot(st, hh):
            return BA[:, VOFF + st * 528 + hh * 132: VOFF + st * 528 + hh * 132 + 132]

        def Vtile(st):
            return BA[:, VOFF + st * 528: VOFF + st * 528 + 528]

        def Pts(st):
            return BA[:, PTOFF + st * 512: PTOFF + (st + 1) * 512]

        def Kbuf(par):
            return BA[:, KOFF + par * 2048: KOFF + (par + 1) * 2048]

        Qb = BA[:, QOFF: QOFF + 1024]
        o1n = BA[:, O1OFF:O1OFF + 1024].bitcast(F32)
        rtmp = BA[:, RTOFF:RTOFF + 1024].bitcast(F32)
        tmpB = BA[:, TBOFF:TBOFF + 512]
        qsq = BA[:, QSOFF:QSOFF + 3072]
        cC = BA[:, CCOFF:CCOFF + 2048]
        cS2 = BA[:, CSOFF:CSOFF + 2048]

        if _full:
            p.v(lambda v: v.memset(BA[:, VOFF:VOFF + 8448], 1.0), inc=VE,
                waits=[(LY, spill_ly)])
        vones_ve = VE.n
        if _full:
            dmaW(SL[7], cC[:, :], ropeC[:, :], waits=[(LY, spill_ly)])
            dmaW(SL[7], cS2[:, :], ropeS2[:, :])
        rope_lw = SL[7].n

        wv_ch = SL[1]
        wk_ch = SL[2]
        wq_ch = SL[3]

        wvslot = wa[:, 0:8192]

        def wkslot(par):
            return hsb[:, par * 2048:(par + 1) * 2048]

        def wqslot(par):
            return hsb[:, 4096 + par * 2048: 4096 + (par + 1) * 2048]

        prevmm = {}
        pt_mm = {}
        pcg = {0: [(VE, vones_ve)], 1: [(VE, vones_ve)]}
        pbg = {0: [], 1: []}
        obg = {0: [], 1: []}
        sag = {0: [], 1: []}
        atb_mm = [0]

        for g4 in range(4 if _full else 0):
            dmaW(wv_ch, wvslot[:, :], wvH[g4],
                 waits=[(MM, prevmm.get("wv", MM.n))])
            nwv = wv_ch.n
            for st in range(16):
                for k in range(NK):
                    wts = []
                    if k == 0:
                        wts = [(wv_ch, nwv)] + pcg[0] + obg[1]
                    p.t(lambda t, st=st, k=k:
                        t.matmul(pC[:, 0:512],
                                 l2x(k, st * 128, (st + 1) * 128),
                                 wvslot[:, k * 512:(k + 1) * 512],
                                 start=(k == 0), stop=(k == NK - 1)),
                        inc=MM, waits=wts)
                p.v(lambda v, st=st:
                    v.tensor_copy(
                        Vtile(st).rearrange("p (h c) -> p h c", c=132)[:, :, 0:128],
                        pC[:, 0:512].rearrange("p (h c) -> p h c", c=128)),
                    inc=VE, waits=[(MM, MM.n)])
                pcg[0] = [(VE, VE.n)]
            v_ve = VE.n

            for hh in range(4):
                h = g4 * 4 + hh
                par = h % 2
                dmaW(wk_ch, wkslot(par)[:, :], wkH[h],
                     waits=[(MM, prevmm.get(("k", par), MM.n))])
                nk_ld = wk_ch.n
                dmaW(wq_ch, wqslot(par)[:, :], wqH[h],
                     waits=[(MM, prevmm.get(("q", par), MM.n))])
                nq_ld = wq_ch.n
                specs = [("k", 0), ("k", 1), ("k", 2), ("k", 3), ("q", 0), ("q", 1)]
                for idx, (kind, sc) in enumerate(specs):
                    pslice = pC[:, (idx % 2) * 512:((idx % 2) + 1) * 512]
                    if kind == "k":
                        dst = Kbuf(par)[:, sc * 512:(sc + 1) * 512]
                        abs0 = sc * 512
                        wch, nld = wk_ch, nk_ld

                        def wsl(k, par=par):
                            return wkslot(par)[:, k * 128:(k + 1) * 128]

                        def rhs(k, sc=sc):
                            return l2x(k, sc * 512, (sc + 1) * 512)
                    else:
                        dst = Qb[:, sc * 512:(sc + 1) * 512]
                        abs0 = 1024 + sc * 512
                        wch, nld = wq_ch, nq_ld

                        def wsl(k, par=par):
                            return wqslot(par)[:, k * 128:(k + 1) * 128]

                        def rhs(k, sc=sc):
                            return l2x(k, 1024 + sc * 512, 1024 + (sc + 1) * 512)
                    for k in range(NK):
                        wts = []
                        if k == 0:
                            wts = [(wch, nld)] + pcg[idx % 2]
                            if idx % 2 == 0:
                                wts += obg[1]
                        p.t(lambda t, k=k, pslice=pslice, wsl=wsl, rhs=rhs:
                            t.matmul(pslice, wsl(k), rhs(k),
                                     start=(k == 0), stop=(k == NK - 1)),
                            inc=MM, waits=wts)
                    mm_p = MM.n
                    p.a(lambda a, dst=dst, pslice=pslice:
                        a.copy(dst, pslice), inc=AC, waits=[(MM, mm_p)])
                    ac_raw = AC.n
                    p.t(lambda t, dst=dst, pslice=pslice:
                        t.matmul(pslice, permb[:, :], dst, start=True, stop=True),
                        inc=MM, waits=[(AC, ac_raw)])
                    mm_perm = MM.n
                    p.v(lambda v, dst=dst, abs0=abs0:
                        v.tensor_mul(tmpB[:, 0:512], dst, cC[:, abs0:abs0 + 512]),
                        waits=[(AC, ac_raw), (SL[7], rope_lw), (MM, atb_mm[0])])
                    p.v(lambda v, pslice=pslice, abs0=abs0:
                        v.tensor_mul(rtmp[:, 0:512], pslice, cS2[:, abs0:abs0 + 512]),
                        inc=VE, waits=[(MM, mm_perm)])
                    pcg[idx % 2] = [(VE, VE.n), (AC, ac_raw)]
                    p.v(lambda v, dst=dst:
                        v.tensor_add(dst, tmpB[:, 0:512], rtmp[:, 0:512]),
                        inc=VE)
                rope_ve = VE.n
                p.a(lambda a, par=par: a.activation(qsq[:, 0:2048], Kbuf(par), AF.Square),
                    inc=AC, waits=[(VE, rope_ve)])
                p.a(lambda a: a.activation(qsq[:, 2048:3072], Qb, AF.Square), inc=AC)
                sq_ac = AC.n
                km = sma[0:1, 512:513]
                k2 = sma[0:1, 513:514]
                qm = sma[0:1, 514:515]
                kmm = sma[0:1, 515:516]
                lnk = sma[0:1, 516:517]
                lnq = sma[0:1, 517:518]
                lsum = sma[0:1, 518:519]
                eB = sma[0:1, 519:520]
                nB = sma[0:1, 520:521]
                p.t(lambda t:
                    t.matmul(pB[0:1, 0:512], onesb[:, 0:1],
                             qsq[:, 0:512], start=True, stop=True),
                    inc=MM, waits=[(AC, sq_ac), (GP, GP.n)] + pbg[0] + pbg[1]
                    + obg[0])
                p.t(lambda t:
                    t.matmul(pB[0:1, 512:1024], onesb[:, 0:1],
                             qsq[:, 512:1024], start=True, stop=True),
                    inc=MM)
                p.v(lambda v: v.reduce_max(km, pB[0:1, 0:1024], AX.X),
                    inc=VE, waits=[(MM, MM.n)])
                p.t(lambda t:
                    t.matmul(pB[0:1, 0:512], onesb[:, 0:1],
                             qsq[:, 1024:1536], start=True, stop=True),
                    inc=MM, waits=[(VE, VE.n)])
                p.t(lambda t:
                    t.matmul(pB[0:1, 512:1024], onesb[:, 0:1],
                             qsq[:, 1536:2048], start=True, stop=True),
                    inc=MM)
                p.v(lambda v: v.reduce_max(k2, pB[0:1, 0:1024], AX.X),
                    inc=VE, waits=[(MM, MM.n)])
                p.g(lambda g: g.tensor_scalar_max(kmm, km, k2[0:1, 0:1]),
                    inc=GP, waits=[(VE, VE.n)])
                p.t(lambda t:
                    t.matmul(pB[0:1, 0:512], onesb[:, 0:1],
                             qsq[:, 2048:2560], start=True, stop=True),
                    inc=MM, waits=[(VE, VE.n)])
                p.t(lambda t:
                    t.matmul(pB[0:1, 512:1024], onesb[:, 0:1],
                             qsq[:, 2560:3072], start=True, stop=True),
                    inc=MM)
                p.v(lambda v: v.reduce_max(qm, pB[0:1, 0:1024], AX.X),
                    inc=VE, waits=[(MM, MM.n)])
                p.a(lambda a: a.activation(lnk, kmm, AF.Ln),
                    waits=[(GP, GP.n), (VE, VE.n)])
                p.a(lambda a: a.activation(lnq, qm, AF.Ln), inc=AC)
                p.g(lambda g: g.tensor_scalar_add(lsum, lnk, lnq[0:1, 0:1]),
                    inc=GP, waits=[(AC, AC.n)])
                p.a(lambda a: a.activation(eB, lsum, AF.Exp, scale=0.5,
                                           bias=cbias[0:1, 1:2]),
                    waits=[(GP, GP.n)])
                p.a(lambda a: a.activation(nB, eB, AF.Copy, scale=-1.0), inc=AC)
                p.a(lambda a: a.copy(lhl[0:1, 2:3], nB), inc=AC)
                p.t(lambda t: t.matmul(pB[:, 512:513], onesr[0:1, :],
                                       lhl[0:1, 2:3], start=True, stop=True),
                    inc=MM, waits=[(AC, AC.n), (VE, VE.n)])
                p.a(lambda a: a.copy(nBc[:, 0:1], pB[:, 512:513]),
                    inc=AC, waits=[(MM, MM.n)])
                p.a(lambda a: a.copy(nBc[:, 1:2], pB[:, 512:513]), inc=AC)
                pbg[0] = [(VE, VE.n)]
                pbg[1] = [(AC, AC.n)]

                atb4 = tmpB
                for qc in range(2):
                    for half in range(2):
                        r0 = half * 64
                        exp_acs = [0] * 16
                        for st in range(16):
                            pair = st // 2
                            psl = pA if (pair % 2 == 0) else pA2
                            col = (st % 2) * 512
                            wts = []
                            if st == 0 and half == 0 and qc == 0:
                                wts.append((VE, rope_ve))
                            if st % 2 == 0:
                                wts += sag[pair % 2]
                            p.t(lambda t, st=st, r0=r0, qc=qc, par=par, psl=psl,
                                col=col:
                                t.matmul(psl[:, col:col + 512],
                                         Kbuf(par)[r0:r0 + 64, st * 128:(st + 1) * 128],
                                         Qb[r0:r0 + 64, qc * 512:(qc + 1) * 512],
                                         start=True, stop=True),
                                inc=MM, waits=wts)
                            if st % 2 == 1:
                                mm_s = MM.n
                                wts2 = [(MM, mm_s)]
                                for stq in (st - 1, st):
                                    if ("pt", stq) in pt_mm:
                                        wts2.append((MM, pt_mm[("pt", stq)]))
                                p.a(lambda a, st=st, half=half, psl=psl:
                                    a.activation(Pts(st - 1).tensor.ap()
                                                 [:, PTOFF + (st - 1) * 512:
                                                  PTOFF + (st + 1) * 512]
                                                 if False else
                                                 BA[:, PTOFF + (st - 1) * 512:
                                                    PTOFF + (st + 1) * 512],
                                                 psl[:, 0:1024], AF.Exp,
                                                 bias=nBc[:, half:half + 1],
                                                 scale=0.125),
                                    inc=AC, waits=wts2)
                                exp_acs[st - 1] = AC.n
                                exp_acs[st] = AC.n
                                sag[pair % 2] = [(AC, AC.n)]
                        for qt in range(4):
                            ob = pB if (qt % 2 == 0) else pC
                            for st in range(16):
                                wts = []
                                if st == 0:
                                    wts += obg[qt % 2]
                                    wts += (pbg[0] if qt % 2 == 0
                                            else pcg[0] + pcg[1])
                                if qt == 0:
                                    wts.append((AC, exp_acs[st]))
                                p.t(lambda t, st=st, qt=qt, hh=hh, ob=ob:
                                    t.matmul(ob[:, 0:129],
                                             Pts(st)[:, qt * 128:(qt + 1) * 128],
                                             Vslot(st, hh)[:, 0:129],
                                             start=(st == 0), stop=(st == 15)),
                                    inc=MM, waits=wts)
                                pt_mm[("pt", st)] = MM.n
                            mm_pv = MM.n
                            if half == 0:
                                p.v(lambda v, ob=ob: v.reciprocal(
                                    zq[:, 0:1], ob[:, 128:129]),
                                    inc=VE, waits=[(MM, mm_pv)])
                                p.wait("v", VE)
                                p.v(lambda v, qt=qt, ob=ob: v.tensor_single_scalar(
                                    o1n[:, qt * 128:(qt + 1) * 128],
                                    ob[:, 0:128], zq[:, 0:1], ALU.mult), inc=VE)
                                obg[qt % 2] = [(VE, VE.n)]
                            else:
                                at0 = att[:, 0:128]
                                at1 = att[:, 128:256]
                                p.v(lambda v, ob=ob: v.reciprocal(
                                    zq[:, 1:2], ob[:, 128:129]),
                                    inc=VE, waits=[(MM, mm_pv)])
                                p.wait("v", VE)
                                p.v(lambda v: v.tensor_mul(zq[:, 2:3], zq[:, 1:2],
                                                           lamc[:, :]), inc=VE)
                                p.wait("v", VE)
                                p.v(lambda v, qt=qt, ob=ob, at0=at0:
                                    v.scalar_tensor_tensor(
                                        at0, ob[:, 0:128], zq[:, 2:3],
                                        o1n[:, qt * 128:(qt + 1) * 128],
                                        ALU.mult, ALU.add), inc=VE)
                                obg[qt % 2] = [(VE, VE.n)]
                                p.v(lambda v, at0=at0, at1=at1:
                                    v.tensor_mul(at1, at0, at0))
                                p.v(lambda v, at1=at1:
                                    v.reduce_sum(zq[:, 3:4], at1, AX.X), inc=VE)
                                p.a(lambda a: a.activation(zq[:, 3:4], zq[:, 3:4],
                                                           AF.Ln, scale=1.0 / 128,
                                                           bias=cbias[:, 0:1]),
                                    waits=[(VE, VE.n)])
                                p.a(lambda a: a.activation(zq[:, 3:4], zq[:, 3:4],
                                                           AF.Exp, scale=-0.5,
                                                           bias=cbias[:, 2:3]), inc=AC)
                                p.v(lambda v, at0=at0: v.tensor_single_scalar(
                                    at0, at0, zq[:, 3:4], ALU.mult),
                                    waits=[(AC, AC.n)])
                                w = [(MM, atb_mm[0])] if qt == 0 else []
                                p.v(lambda v, at0=at0, qt=qt:
                                    v.tensor_mul(atb4[:, qt * 128:(qt + 1) * 128],
                                                 at0, gamb[:, :]),
                                    inc=VE, waits=w)
                        if half == 1:
                            pass
                    att_ve2 = VE.n
                    cp_ve = {}
                    for qt in range(4):
                        tp = pA if qt % 2 == 0 else pA2
                        wts = [(VE, att_ve2), (AC, AC.n)] if qt == 0 else []
                        if qt >= 2:
                            wts = [(VE, cp_ve[qt - 2])]
                        p.t(lambda t, qt=qt, tp=tp:
                            t.transpose(tp[:, 0:64].bitcast(BF16)[:, 0:128],
                                        atb4[:, qt * 128:(qt + 1) * 128],
                                        identb[:, :]),
                            inc=MM, waits=wts)
                        p.v(lambda v, h=h, qc=qc, qt=qt, tp=tp:
                            v.tensor_copy(
                                mid[:, h * 1024 + qc * 512 + qt * 128:
                                    h * 1024 + qc * 512 + (qt + 1) * 128],
                                tp[:, 0:64].bitcast(BF16)[:, 0:128]),
                            inc=VE, waits=[(MM, MM.n)])
                        cp_ve[qt] = VE.n
                    atb_mm[0] = MM.n
                    sag[0] = sag[0] + [(VE, VE.n)]
                    sag[1] = sag[1] + [(VE, VE.n)]
                prevmm[("k", par)] = MM.n
                prevmm[("q", par)] = MM.n
            prevmm["wv"] = MM.n
        attn_ve = VE.n
        if os.environ.get("ATH_DBG") == "attn":
            dmaW(SL[0], dbg2.bitcast(BF16)[:, 0:NK * TOK], mid[:, :],
                 waits=[(VE, attn_ve)])
            dmaY(dbgd[299:300, 0:1], lamc[0:1, 0:1], waits=[(VE, attn_ve)])
            dmaY(dbgd[299:300, 1:2], lamc[64:65, 0:1], waits=[(VE, attn_ve)])
            dmaY(dbgd[298:299, 0:8], sma[0:1, 384:392], waits=[(VE, attn_ve)])
            dmaY(dbgd[297:298, 0:64], sma[0:1, 0:64], waits=[(VE, attn_ve)])
            dmaY(dbgd[296:297, 0:64], sma[0:1, 256:320], waits=[(VE, attn_ve)])

        if _full:
            dmaY(bigA[:, :], xspill[:, :],
                 waits=[(VE, attn_ve), (MM, MM.n), (AC, AC.n), (GP, GP.n)])
        reload_ly = LY.n
        wo_ch = SL[8]
        wo_prev = {}
        for fo in range(NDO if PH >= 6 else 0):
            sl = fo % 2

            def wos(sl=sl):
                return wa[:, sl * 2048:(sl + 1) * 2048]
            dmaW(wo_ch, wos()[:, :], woH[fo],
                 waits=[(MM, wo_prev.get(sl, MM.n))])
            nwo = wo_ch.n
            for qc in range(2):
                pbank = pB if (qc % 2 == 0) else pC
                for hh2 in range(NH):
                    wts = []
                    if hh2 == 0:
                        wts = [(wo_ch, nwo), (VE, VE.n)]
                    p.t(lambda t, qc=qc, hh2=hh2, pbank=pbank, wos=wos:
                        t.matmul(pbank[:, 0:512],
                                 wos()[:, hh2 * 128:(hh2 + 1) * 128],
                                 mid[:, hh2 * 1024 + qc * 512:
                                     hh2 * 1024 + (qc + 1) * 512],
                                 start=(hh2 == 0), stop=(hh2 == NH - 1)),
                        inc=MM, waits=wts)
                p.v(lambda v, fo=fo, qc=qc, pbank=pbank:
                    v.tensor_add(xr(fo, qc * 512, (qc + 1) * 512),
                                 pbank[:, 0:512],
                                 xr(fo, qc * 512, (qc + 1) * 512)),
                    inc=VE, waits=[(MM, MM.n), (LY, reload_ly)])
            wo_prev[sl] = MM.n

        if PH >= 7:
            emit_ln(lambda k: lnx(k))
            emit_ffn(lambda k: lnx(k))

        midf2 = mid.ap().bitcast(F32)

        def stage(par):
            return midf2[:, par * 2048:(par + 1) * 2048]
        st_ch = SL[9]
        mm_out0 = MM.n
        for tt in range(8):
            par = tt % 2
            for k in range(NK):
                pk = k % 2
                wts = [(VE, VE.n)]
                if tt >= 2 and k == 0:
                    wts.append((st_ch, st_ch.n))
                if tt < 2 and k == 0:
                    wts.append((MM, mm_out0))
                    wts.append((LY, LY.n))
                p.t(lambda t, tt=tt, k=k, pk=pk:
                    t.transpose(pC[:, pk * 512:pk * 512 + 128],
                                xr(k, tt * 128, (tt + 1) * 128), identf[:, :]),
                    inc=MM, waits=wts)
                p.v(lambda v, tt=tt, k=k, pk=pk, par=par:
                    v.tensor_copy(stage(par)[:, k * 128:(k + 1) * 128],
                                  pC[:, pk * 512:pk * 512 + 128]),
                    inc=VE, waits=[(MM, MM.n)])
            dmaY(out_d[tt * 128:(tt + 1) * 128, :], stage(par)[:, :],
                 waits=[(VE, VE.n)], ch=st_ch)
        if os.environ.get("ATH_DBG") == "zsc":
            dmaY(out_d[0:8, 0:1024], zscd[:, :], waits=[(st_ch, st_ch.n)], ch=st_ch)
        if os.environ.get("ATH_DBG") == "gate":
            dmaY(out_d[0:512, :], dbgd[:, :], waits=[(st_ch, st_ch.n)], ch=st_ch)
        if os.environ.get("ATH_DBG") == "attn":
            dmaY(out_d[512:832, :], dbgd[0:320, :],
                 waits=[(st_ch, st_ch.n)], ch=st_ch)
            dmaY(out_d.rearrange("a b -> (a b)").bitcast(BF16)
                 .rearrange("(p c) -> p c", p=128)[:, 0:NK * TOK],
                 dbg2.bitcast(BF16)[:, 0:NK * TOK],
                 waits=[(st_ch, st_ch.n), (SL[0], SL[0].n)], ch=st_ch)
        p.wait("y", st_ch)
        p.wait("g", st_ch)

        p.check()

        with nc.Block() as block:
            block.tensor(lambda te: ([f(te) for f in p.q["t"]], None)[1])
            block.vector(lambda ve: ([f(ve) for f in p.q["v"]], None)[1])
            block.scalar(lambda ac: ([f(ac) for f in p.q["a"]], None)[1])
            block.gpsimd(lambda gp: ([f(gp) for f in p.q["g"]], None)[1])
            block.sync(lambda sy: ([f(sy) for f in p.q["y"]], None)[1])

    return nc


def _host_layouts(inputs):
    Wq = np.asarray(inputs["L0_attn_Wq"])
    Wk = np.asarray(inputs["L0_attn_Wk"])
    Wv = np.asarray(inputs["L0_attn_Wv"])
    Wo = np.asarray(inputs["L0_attn_Wo"])
    w1 = np.asarray(inputs["L0_ffn_w1"])
    w2 = np.asarray(inputs["L0_ffn_w2"])
    w3 = np.asarray(inputs["L0_ffn_w3"])
    sw1 = np.asarray(inputs["L0_ffn_sw1"])
    sw2 = np.asarray(inputs["L0_ffn_sw2"])
    sw3 = np.asarray(inputs["L0_ffn_sw3"])
    gw = np.asarray(inputs["L0_ffn_gate_w"])
    C = np.ascontiguousarray
    d = {}
    d["wkH"] = C(Wk.reshape(16, 128, 16, 128).transpose(0, 3, 2, 1).reshape(16, 128, 2048))
    d["wqH"] = C(Wq.reshape(16, 128, 16, 128).transpose(0, 3, 2, 1).reshape(16, 128, 2048))
    d["wvH"] = C(Wv.reshape(4, 512, 16, 128).transpose(0, 3, 2, 1).reshape(4, 128, 8192))
    d["woH"] = C(Wo.reshape(16, 128, 16, 128).transpose(0, 3, 2, 1).reshape(16, 128, 2048))
    d["gateH"] = C(gw.T.reshape(16, 128, 8).transpose(1, 0, 2).reshape(128, 128))
    d["w1H"] = C(w1.reshape(8, 8, 128, 16, 128).transpose(0, 1, 4, 3, 2)
                 .reshape(8, 8, 128, 2048))
    d["w3H"] = C(w3.reshape(8, 8, 128, 16, 128).transpose(0, 1, 4, 3, 2)
                 .reshape(8, 8, 128, 2048))
    d["w2H"] = C(w2.reshape(8, 16, 128, 8, 128).transpose(0, 1, 4, 3, 2)
                 .reshape(8, 16, 128, 1024))
    d["sw1H"] = C(sw1.reshape(8, 128, 16, 128).transpose(0, 3, 2, 1).reshape(8, 128, 2048))
    d["sw3H"] = C(sw3.reshape(8, 128, 16, 128).transpose(0, 3, 2, 1).reshape(8, 128, 2048))
    d["sw2H"] = C(sw2.reshape(16, 128, 8, 128).transpose(0, 3, 2, 1).reshape(16, 128, 1024))
    d["gamma"] = C(np.asarray(inputs["L0_attn_gamma"]))
    d["ebias"] = C(np.asarray(inputs["L0_ffn_expert_bias"]))
    d["lq1"] = C(np.asarray(inputs["L0_attn_lambda_q1"]))
    d["lk1"] = C(np.asarray(inputs["L0_attn_lambda_k1"]))
    d["lq2"] = C(np.asarray(inputs["L0_attn_lambda_q2"]))
    d["lk2"] = C(np.asarray(inputs["L0_attn_lambda_k2"]))
    d["identf"] = np.eye(128, dtype=np.float32)
    d["identb"] = np.eye(128, dtype=BF)
    permv = np.zeros((128, 128), dtype=BF)
    for i in range(128):
        permv[i ^ 1, i] = 1.0
    d["permb"] = permv
    d["iota1000"] = np.broadcast_to((1000.0 + np.arange(8)).astype(np.float32),
                                    (128, 8)).copy()
    return d


def _rope_tables(positions):
    HDh = 32
    angle = 1.0 / 10000 ** np.linspace(0.0, 1.0, HDh, dtype=np.float32)
    freqs = positions[None, :].astype(np.float32) * angle[:, None]
    cos = np.cos(freqs).astype(np.float32)
    sin = np.sin(freqs).astype(np.float32)
    Cp = np.empty((128, 2048), np.float32)
    S2 = np.empty((128, 2048), np.float32)
    for pp in range(128):
        i = (pp % 64) // 2
        Cp[pp] = cos[i]
        S2[pp] = -sin[i] if (pp % 2 == 0) else sin[i]
    return Cp, S2


def kernel(**inputs):
    x = np.asarray(inputs["L0_attn_x"])
    shared = _host_layouts(inputs)
    nc = build_program()
    in_maps = []
    for c in range(8):
        b, hf = c // 2, c % 2
        xo = np.ascontiguousarray(x[b, hf * 1024:(hf + 1) * 1024])
        xp = np.ascontiguousarray(x[b, (1 - hf) * 1024:(2 - hf) * 1024])
        peer_pos = np.arange((1 - hf) * 1024, (2 - hf) * 1024)
        own_pos = np.arange(hf * 1024, (hf + 1) * 1024)
        Cp, S2 = _rope_tables(np.concatenate([peer_pos, own_pos]))
        m = dict(shared)
        m["x_own"] = xo
        m["x_peer"] = xp
        m["ropeC"] = Cp
        m["ropeS2"] = S2
        in_maps.append(m)
    res = run_bass_kernel_spmd(nc, in_maps, core_ids=list(range(8)))
    out = np.empty((4, 2048, 2048), np.float32)
    for c, r in enumerate(res.results):
        b, hf = c // 2, c % 2
        out[b, hf * 1024:(hf + 1) * 1024] = r["out"]
    return out


# version 05
# speedup vs baseline: 1.0514x; 1.0514x over previous
import math
import os
import numpy as np
import ml_dtypes
import concourse.bass as bass
import concourse.mybir as mybir
from concourse.bass_utils import run_bass_kernel_spmd
from contextlib import ExitStack

F32 = mybir.dt.float32
BF16 = mybir.dt.bfloat16
AF = mybir.ActivationFunctionType
ALU = mybir.AluOpType
AX = mybir.AxisListType
BF = ml_dtypes.bfloat16

D = 2048
TOK = 1024
S = 2048
NH = 16
NE = 8
HID = 1024
NK = 16
NJ = 8
NDO = 16
LAMB_INIT = 0.8 - 0.6 * math.exp(-0.3 * 12)
EPS = 1e-5
LN8 = math.log(8.0)
LN_OMLI = math.log(1.0 - LAMB_INIT)


class Ch:
    def __init__(self, h, step=1):
        self.h = h
        self.n = 0
        self.step = step


class Prog:
    def __init__(self):
        self.q = {e: [] for e in "tvagy"}
        self.waits = []

    def _emit(self, eng, fn, inc, waits):
        for (c, n) in waits:
            if n > 0:
                self.waits.append((c, n))
        step_amt = None
        if inc is not None:
            inc.n += inc.step
            step_amt = inc.step

        def run(e, fn=fn, inc=inc, waits=tuple(waits), step_amt=step_amt):
            for (c, n) in waits:
                if n > 0:
                    e.wait_ge(c.h, n)
            ins = fn(e)
            if inc is not None:
                ins.then_inc(inc.h, step_amt)
        self.q[eng].append(run)

    def t(self, fn, inc=None, waits=()):
        self._emit("t", fn, inc, waits)

    def v(self, fn, inc=None, waits=()):
        self._emit("v", fn, inc, waits)

    def a(self, fn, inc=None, waits=()):
        self._emit("a", fn, inc, waits)

    def g(self, fn, inc=None, waits=()):
        self._emit("g", fn, inc, waits)

    def y(self, fn, inc=None, waits=()):
        self._emit("y", fn, inc, waits)

    def wait(self, eng, ch, n=None):
        n = ch.n if n is None else n
        if n <= 0:
            return
        self.waits.append((ch, n))

        def run(e, ch=ch, n=n):
            e.wait_ge(ch.h, n)
        self.q[eng].append(run)

    def check(self):
        for (c, n) in self.waits:
            assert n <= c.n, f"wait {n} > final {c.n}"


def build_program():
    nc = bass.Bass("TRN2", target_bir_lowering=False)

    def din(name, shape, dt=F32):
        return nc.dram_tensor(name, shape, dt, kind="ExternalInput").ap()

    x_own = din("x_own", [TOK, D])
    x_peer = din("x_peer", [TOK, D])
    ropeC = din("ropeC", [128, S])
    ropeS2 = din("ropeS2", [128, S])
    wkH = din("wkH", [NH, 128, NK * 128])
    wqH = din("wqH", [NH, 128, NK * 128])
    wvH = din("wvH", [4, 128, NK * 512])
    woH = din("woH", [NDO, 128, NH * 128])
    gateH = din("gateH", [128, 128])
    w1H = din("w1H", [NE, NJ, 128, NK * 128])
    w3H = din("w3H", [NE, NJ, 128, NK * 128])
    w2H = din("w2H", [NE, NDO, 128, NJ * 128])
    sw1H = din("sw1H", [NJ, 128, NK * 128])
    sw3H = din("sw3H", [NJ, 128, NK * 128])
    sw2H = din("sw2H", [NDO, 128, NJ * 128])
    gamma = din("gamma", [128])
    ebias = din("ebias", [8])
    lq1 = din("lq1", [64])
    lk1 = din("lk1", [64])
    lq2 = din("lq2", [64])
    lk2 = din("lk2", [64])
    identf_d = din("identf", [128, 128])
    identb_d = din("identb", [128, 128], BF16)
    permb_d = din("permb", [128, 128], BF16)
    iota_d = din("iota1000", [128, 8])

    out_d = nc.dram_tensor("out", [TOK, D], F32, kind="ExternalOutput").ap()
    xspill = nc.dram_tensor("xspill", [128, NK * TOK], F32, kind="Internal").ap()
    dscr = nc.dram_tensor("dscr", [8192], F32, kind="Internal").ap()
    zscd = nc.dram_tensor("zscd", [8, TOK], F32, kind="Internal").ap()
    dbgd = nc.dram_tensor("dbgd", [512, 2048], F32, kind="Internal").ap()
    dbg2 = nc.dram_tensor("dbg2", [128, NK * TOK], F32, kind="Internal").ap()

    def drow(c0, c1):
        return dscr[c0:c1].rearrange("(o j) -> o j", o=1)

    with ExitStack() as ctx:
        E = ctx.enter_context
        bigA = E(nc.sbuf_tensor("bigA", [128, NK * TOK], F32))
        ln2x = E(nc.sbuf_tensor("ln2x", [128, NK * S], BF16))
        mid = E(nc.sbuf_tensor("mid", [128, NK * TOK], BF16))
        wa = E(nc.sbuf_tensor("wa", [128, 8192], BF16))
        hsb = E(nc.sbuf_tensor("hsb", [128, NJ * TOK], BF16))
        sma = E(nc.sbuf_tensor("sma", [128, 2048], F32))
        identf = E(nc.sbuf_tensor("identf_s", [128, 128], F32))
        identb = E(nc.sbuf_tensor("identb_s", [128, 128], BF16))
        permb = E(nc.sbuf_tensor("permb_s", [128, 128], BF16))
        gamb = E(nc.sbuf_tensor("gamb", [128, 128], F32))
        iot = E(nc.sbuf_tensor("iot", [128, 8], F32))
        ebb = E(nc.sbuf_tensor("ebb", [128, 8], F32))
        onesb = E(nc.sbuf_tensor("onesb", [128, 1], BF16))
        gateS = E(nc.sbuf_tensor("gateS", [128, 128], BF16))
        lamc = E(nc.sbuf_tensor("lamc", [128, 1], F32))
        nBc = E(nc.sbuf_tensor("nBc", [128, 2], F32))
        zq = E(nc.sbuf_tensor("zq", [128, 4], F32))
        att = E(nc.sbuf_tensor("att", [128, 384], F32))
        cbias = E(nc.sbuf_tensor("cbias", [128, 4], F32))
        onesr = E(nc.sbuf_tensor("onesr", [1, 128], BF16))
        lhl = E(nc.sbuf_tensor("lhl", [1, 4], BF16))

        pA = E(nc.psum_tensor("pA", [128, 1024], F32))
        pA2 = E(nc.psum_tensor("pA2", [128, 1024], F32))
        pB = E(nc.psum_tensor("pB", [128, 1024], F32))
        pC = E(nc.psum_tensor("pC", [128, 1024], F32))

        sem_names = ["LW", "LY", "MM", "VE", "AC", "GP",
                     "s0", "s1", "s2", "s3", "s4", "s5", "s6", "s7", "s8", "s9"]
        hs = {n: E(nc.semaphore(n)) for n in sem_names}
        LW = Ch(hs["LW"], 16)
        LY = Ch(hs["LY"], 16)
        MM = Ch(hs["MM"], 1)
        VE = Ch(hs["VE"], 1)
        AC = Ch(hs["AC"], 1)
        GP = Ch(hs["GP"], 1)
        SL = [Ch(hs[f"s{i}"], 16) for i in range(10)]

        p = Prog()

        def dmaW(ch, out_ap, in_ap, waits=()):
            ch.n += 16
            n = ch.n

            def run(g, o=out_ap, i=in_ap, waits=tuple(waits), ch=ch):
                for (c, w) in waits:
                    if w > 0:
                        g.wait_ge(c.h, w)
                g.dma_start(o, i).then_inc(ch.h, 16)
            for (c, w) in waits:
                if w > 0:
                    p.waits.append((c, w))
            p.q["g"].append(run)

        def dmaY(out_ap, in_ap, waits=(), ch=None):
            ch = LY if ch is None else ch
            ch.n += 16
            n = ch.n

            def run(y, o=out_ap, i=in_ap, waits=tuple(waits), ch=ch):
                for (c, w) in waits:
                    if w > 0:
                        y.wait_ge(c.h, w)
                y.dma_start(o, i).then_inc(ch.h, 16)
            for (c, w) in waits:
                if w > 0:
                    p.waits.append((c, w))
            p.q["y"].append(run)

        def xr(k, c0=0, c1=TOK):
            return bigA[:, k * TOK + c0: k * TOK + c1]

        def lnx(k):
            return mid[:, k * TOK:(k + 1) * TOK]

        def l2x(k, c0=0, c1=S):
            return ln2x[:, k * S + c0: k * S + c1]

        PH = int(os.environ.get("ATH_PH", "9"))
        STARTUP = PH >= -1
        dmaY(identf[:, :], identf_d[:, :])
        dmaY(identb[:, :], identb_d[:, :])
        dmaY(permb[:, :], permb_d[:, :])
        dmaY(iot[:, :], iota_d[:, :])
        if STARTUP:
            dmaY(gamb[:, :], gamma.rearrange("(o j) -> o j", o=1).broadcast_to([128, 128]))
            dmaY(ebb[:, :], ebias.rearrange("(o j) -> o j", o=1).broadcast_to([128, 8]))
            dmaY(sma[0:1, 0:64], lq1.rearrange("(o j) -> o j", o=1))
            dmaY(sma[0:1, 64:128], lk1.rearrange("(o j) -> o j", o=1))
            dmaY(sma[0:1, 128:192], lq2.rearrange("(o j) -> o j", o=1))
            dmaY(sma[0:1, 192:256], lk2.rearrange("(o j) -> o j", o=1))
        dmaW(LW, gateS[:, :], gateH[:, :])
        gate_lw = LW.n
        p.v(lambda v: v.memset(onesb[:, :], 1.0), inc=VE)
        p.v(lambda v: v.memset(onesr[:, :], 1.0), inc=VE)
        p.v(lambda v: v.memset(cbias[:, 0:1], EPS), inc=VE)
        p.v(lambda v: v.memset(cbias[:, 1:2], -LN8), inc=VE)
        p.v(lambda v: v.memset(cbias[:, 2:3], LN_OMLI), inc=VE)

        consts_ly = LY.n
        if not STARTUP:
            p.v(lambda v: v.memset(sma[0:1, 0:400], 0.5), inc=VE, waits=[(LY, consts_ly)])
        if STARTUP:
            p.v(lambda v: v.tensor_mul(sma[0:1, 256:320], sma[0:1, 0:64], sma[0:1, 64:128]),
                inc=VE, waits=[(LY, consts_ly)])
            p.v(lambda v: v.tensor_mul(sma[0:1, 320:384], sma[0:1, 128:192], sma[0:1, 192:256]),
                inc=VE)
            p.wait("v", VE)
            p.v(lambda v: v.reduce_sum(sma[0:1, 384:385], sma[0:1, 256:320], AX.X),
                inc=VE)
            p.v(lambda v: v.reduce_sum(sma[0:1, 385:386], sma[0:1, 320:384], AX.X), inc=VE)
            p.a(lambda a: a.activation(sma[0:1, 386:387], sma[0:1, 384:385], AF.Exp),
                waits=[(VE, VE.n)])
            p.a(lambda a: a.activation(sma[0:1, 387:388], sma[0:1, 385:386], AF.Exp), inc=AC)
            p.v(lambda v: v.tensor_sub(sma[0:1, 388:389], sma[0:1, 387:388], sma[0:1, 386:387]),
                inc=VE, waits=[(AC, AC.n)])
            p.wait("v", VE)
            p.v(lambda v: v.tensor_scalar_add(sma[0:1, 389:390], sma[0:1, 388:389], -LAMB_INIT),
                inc=VE)
            p.a(lambda a: a.copy(lhl[0:1, 0:1], sma[0:1, 389:390]),
                inc=AC, waits=[(VE, VE.n)])
            p.v(lambda v: v.tensor_sub(sma[0:1, 390:391], sma[0:1, 389:390],
                                       lhl[0:1, 0:1]),
                inc=VE, waits=[(AC, AC.n)])
            p.a(lambda a: a.copy(lhl[0:1, 1:2], sma[0:1, 390:391]),
                inc=AC, waits=[(VE, VE.n)])
            p.t(lambda t: t.matmul(pB[:, 0:1], onesr[0:1, :], lhl[0:1, 0:1],
                                   start=True, stop=False),
                waits=[(AC, AC.n), (VE, VE.n)])
            p.t(lambda t: t.matmul(pB[:, 0:1], onesr[0:1, :], lhl[0:1, 1:2],
                                   start=False, stop=True), inc=MM)
            p.a(lambda a: a.copy(lamc[:, :], pB[:, 0:1]),
                inc=AC, waits=[(MM, MM.n)])

        def emit_ln(dst_fn):
            Tn = TOK
            hb = hsb

            def xb_slot(par):
                return hb[:, par * Tn:(par + 1) * Tn]

            def xsq_slot(par):
                return hb[:, (2 + par) * Tn:(3 + par) * Tn]
            mm0 = MM.n
            ve_in0 = VE.n
            guard_mm = MM.n
            for k in range(NK):
                par = k % 2
                w = []
                if k < 2:
                    w = [(VE, ve_in0), (MM, guard_mm)]
                else:
                    w = [(MM, mm0 + 4 * (k - 1))]
                p.a(lambda a, k=k, par=par: a.copy(xb_slot(par), xr(k)),
                    inc=AC, waits=w)
                p.a(lambda a, k=k, par=par:
                    a.activation(xsq_slot(par), xr(k), AF.Square), inc=AC)
                for tc in range(2):
                    c0, c1 = tc * 512, (tc + 1) * 512
                    wm = [(AC, AC.n)]
                    if k == 0 and tc == 0:
                        wm.append((VE, VE.n))
                        wm.append((AC, AC.n))
                    p.t(lambda t, par=par, c0=c0, c1=c1, k=k:
                        t.matmul(pB[0:1, c0:c1], onesb[:, :], xb_slot(par)[:, c0:c1],
                                 start=(k == 0), stop=(k == NK - 1)),
                        inc=MM, waits=wm)
                    p.t(lambda t, par=par, c0=c0, c1=c1, k=k:
                        t.matmul(pA[0:1, c0:c1], onesb[:, :], xsq_slot(par)[:, c0:c1],
                                 start=(k == 0), stop=(k == NK - 1)),
                        inc=MM)
            mrow = sma[0:1, 1024:1024 + Tn]
            vrow = sma[0:1, 0:Tn]
            hb16 = hsb.ap()
            m_bf = hb16[0:1, 4096:4096 + Tn]
            rhi = hb16[0:1, 5120:5120 + Tn]
            rlo = hb16[0:1, 6144:6144 + Tn]
            p.a(lambda a: a.activation(mrow, pB[0:1, 0:Tn], AF.Copy, scale=1.0 / D),
                inc=AC, waits=[(MM, MM.n)])
            p.a(lambda a: a.copy(m_bf, mrow), inc=AC)
            p.v(lambda v: v.tensor_mul(vrow, mrow, mrow), waits=[(AC, AC.n)])
            p.v(lambda v: v.scalar_tensor_tensor(vrow, pA[0:1, 0:Tn], 1.0 / D,
                                                 vrow, ALU.mult, ALU.subtract), inc=VE)
            p.a(lambda a: a.activation(mrow, vrow, AF.Ln, bias=cbias[0:1, 0:1]),
                waits=[(VE, VE.n)])
            p.a(lambda a: a.activation(mrow, mrow, AF.Exp, scale=-0.5), inc=AC)
            p.a(lambda a: a.copy(rhi, mrow), inc=AC)
            p.v(lambda v: v.tensor_sub(vrow, mrow, rhi), inc=VE, waits=[(AC, AC.n)])
            p.a(lambda a: a.copy(rlo, vrow), inc=AC, waits=[(VE, VE.n)])
            mm_bc0 = MM.n
            for cc in range(2):
                p.t(lambda t, cc=cc:
                    t.matmul(pB[:, cc * 512:(cc + 1) * 512], onesr[0:1, :],
                             m_bf[0:1, cc * 512:(cc + 1) * 512],
                             start=True, stop=True),
                    inc=MM, waits=[(AC, AC.n), (VE, VE.n)] if cc == 0 else [])
                p.t(lambda t, cc=cc:
                    t.matmul(pA[:, cc * 512:(cc + 1) * 512], onesr[0:1, :],
                             rhi[0:1, cc * 512:(cc + 1) * 512],
                             start=True, stop=False),
                    inc=MM)
                p.t(lambda t, cc=cc:
                    t.matmul(pA[:, cc * 512:(cc + 1) * 512], onesr[0:1, :],
                             rlo[0:1, cc * 512:(cc + 1) * 512],
                             start=False, stop=True),
                    inc=MM)
            mm_bc = MM.n
            for k in range(NK):
                w = [(MM, mm_bc)] if k == 0 else []
                p.v(lambda v, k=k: v.tensor_sub(dst_fn(k), xr(k), pB[:, 0:Tn]),
                    inc=VE, waits=w)
            for k in range(NK):
                p.v(lambda v, k=k: v.tensor_mul(dst_fn(k), dst_fn(k), pA[:, 0:Tn]),
                    inc=VE)

        def emit_load_x(src_dram):
            midf = mid.ap().bitcast(F32)

            def xslot(par):
                return midf[:, par * 2048:(par + 1) * 2048]
            ly0 = LY.n
            mm0 = MM.n
            ve0 = VE.n
            guard = max(lnx_free_mm[0], 1) if lnx_free_mm[0] else MM.n
            for tt in range(8):
                par = tt % 2
                w = [(MM, mm0 + 16 * (tt - 1))] if tt >= 2 else [(MM, guard)]
                dmaY(xslot(par), src_dram[tt * 128:(tt + 1) * 128, :], waits=w)
                for k in range(NK):
                    pk = k % 2
                    i = 16 * tt + k
                    wv = []
                    if k == 0:
                        wv.append((LY, ly0 + 16 * (tt + 1)))
                    if i >= 2:
                        wv.append((VE, ve0 + i - 1))
                    else:
                        wv.append((VE, ve0))
                    p.t(lambda t, tt=tt, k=k, pk=pk, par=par:
                        t.transpose(pC[:, pk * 512:pk * 512 + 128],
                                    xslot(par)[:, k * 128:(k + 1) * 128], identf[:, :]),
                        inc=MM, waits=wv)
                    p.v(lambda v, tt=tt, k=k, pk=pk:
                        v.tensor_copy(xr(k, tt * 128, (tt + 1) * 128),
                                      pC[:, pk * 512:pk * 512 + 128]),
                        inc=VE, waits=[(MM, MM.n)])

        ffn_dbg = [True]
        spill_ve8 = [0]
        spill_ve16 = [0]
        lnx_free_mm = [0]

        def emit_ffn(lnsrc):
            Tn = TOK
            ntc = 2
            guard_mm = MM.n
            for k in range(NK):
                for tc in range(ntc):
                    w = []
                    if k == 0 and tc == 0:
                        w = [(LW, gate_lw), (VE, VE.n), (AC, AC.n)]
                    p.t(lambda t, k=k, tc=tc:
                        t.matmul(pB[0:8, tc * 512:(tc + 1) * 512],
                                 gateS[:, k * 8:(k + 1) * 8],
                                 lnsrc(k)[:, tc * 512:(tc + 1) * 512],
                                 start=(k == 0), stop=(k == NK - 1)),
                        inc=MM, waits=w)
            p.a(lambda a: a.activation(sma[0:8, 0:Tn], pB[0:8, 0:Tn], AF.Sigmoid),
                inc=AC, waits=[(MM, MM.n)])
            if os.environ.get("ATH_DBG") == "gate" and ffn_dbg[0]:
                dmaY(dbgd[0:8, 0:Tn], sma[0:8, 0:Tn], waits=[(AC, AC.n)])
            zfm = hsb.ap().bitcast(F32)
            bt = sma[:, 1024:1032]
            mx = sma[:, 1032:1040]
            ug = sma[:, 1040:1048]
            sg = sma[:, 1048:1056]
            t1g = sma[:, 1056:1064]
            scl = sma[:, 1064:1072]
            selg = sma[:, 1072:1080]
            zsg = sma[:, 1080:1088]
            for tt in range(Tn // 128):
                w = [(AC, AC.n)]
                if tt > 0:
                    w.append((VE, VE.n))
                p.t(lambda t, tt=tt:
                    t.transpose(pA[:, 0:8],
                                sma[0:8, tt * 128:(tt + 1) * 128], identf[0:8, 0:8]),
                    inc=MM, waits=w)
                mmt = MM.n
                p.v(lambda v: v.tensor_add(bt, pA[:, 0:8], ebb[:, :]),
                    inc=VE, waits=[(MM, mmt), (GP, GP.n)])
                p.wait("v", VE)
                p.v(lambda v: v.max(mx, bt), inc=VE)
                p.g(lambda g: g.tensor_single_scalar(ug, bt, mx[:, 1:2], ALU.subtract),
                    inc=GP, waits=[(VE, VE.n)])
                p.a(lambda a: a.activation(sg, ug, AF.Sign),
                    inc=AC, waits=[(GP, GP.n)])
                p.v(lambda v: v.tensor_scalar_add(t1g, sg, 1.0),
                    waits=[(AC, AC.n)])
                p.v(lambda v: v.tensor_scalar_max(scl, sg, 0.0), inc=VE)
                p.wait("v", VE)
                p.v(lambda v: v.tensor_sub(selg, t1g, scl), inc=VE)
                p.wait("v", VE)
                p.v(lambda v: v.tensor_mul(zsg, selg, pA[:, 0:8]), inc=VE)
                p.t(lambda t, tt=tt:
                    t.transpose(pA2[0:8, 0:128], zsg, identf[:, :]),
                    inc=MM, waits=[(VE, VE.n)])
                p.a(lambda a, tt=tt: a.copy(zfm[0:8, tt * 128:(tt + 1) * 128],
                                            pA2[0:8, 0:128]),
                    inc=AC, waits=[(MM, MM.n)])
            dmaY(zscd[:, 0:Tn], zfm[0:8, 0:Tn], waits=[(AC, AC.n)])
            zout_ly = LY.n

            def w13s(i):
                return wa[:, (i % 3) * 2048:((i % 3) + 1) * 2048]

            def w2s(i):
                return wa[:, 6144 + (i % 2) * 1024: 6144 + ((i % 2) + 1) * 1024]
            zc = sma[:, 0:Tn]
            ubuf = sma[:, 1024:1536]
            vbuf = sma[:, 1536:2048]
            w13_ch = [SL[1], SL[2], SL[3]]
            w2_ch = [SL[4], SL[5]]
            zc_ch = SL[6]
            gidx = [0]
            w2idx = [0]
            slot_mm = {}
            zc_ve_prev = [VE.n]
            first_h = [True]
            slot_ve = {0: VE.n, 1: VE.n}
            epi_ve = {}

            for e in range(NE + 1):
                shared = (e == NE)
                if not shared:
                    dmaY(zc, zscd[e:e + 1, 0:Tn].broadcast_to([128, Tn]),
                         waits=[(LY, zout_ly), (VE, zc_ve_prev[0])], ch=zc_ch)
                zc_ld = zc_ch.n
                for j in range(NJ):
                    i1s = gidx[0]
                    i3s = gidx[0] + 1
                    gidx[0] += 2
                    src1 = sw1H[j] if shared else w1H[e, j]
                    src3 = sw3H[j] if shared else w3H[e, j]
                    ch1 = w13_ch[i1s % 3]
                    ch3 = w13_ch[i3s % 3]
                    dmaW(ch1, w13s(i1s)[:, :], src1,
                         waits=[(MM, slot_mm.get(i1s % 3, guard_mm))])
                    n1 = ch1.n
                    dmaW(ch3, w13s(i3s)[:, :], src3,
                         waits=[(MM, slot_mm.get(i3s % 3, guard_mm))])
                    n3 = ch3.n
                    for tc in range(ntc):
                        for k in range(NK):
                            wts = []
                            if k == 0:
                                wts = [(ch1, n1), (VE, slot_ve[tc])]
                            p.t(lambda t, tc=tc, k=k, i1s=i1s:
                                t.matmul(pA[:, tc * 512:(tc + 1) * 512],
                                         w13s(i1s)[:, k * 128:(k + 1) * 128],
                                         lnsrc(k)[:, tc * 512:(tc + 1) * 512],
                                         start=(k == 0), stop=(k == NK - 1)),
                                inc=MM, waits=wts)
                        for k in range(NK):
                            wts = [(ch3, n3)] if k == 0 else []
                            p.t(lambda t, tc=tc, k=k, i3s=i3s:
                                t.matmul(pA2[:, tc * 512:(tc + 1) * 512],
                                         w13s(i3s)[:, k * 128:(k + 1) * 128],
                                         lnsrc(k)[:, tc * 512:(tc + 1) * 512],
                                         start=(k == 0), stop=(k == NK - 1)),
                                inc=MM, waits=wts)
                        mm_a = MM.n
                        hslot = hsb[:, j * TOK + tc * 512: j * TOK + (tc + 1) * 512]
                        if shared:
                            wts = [(MM, mm_a)]
                            p.a(lambda a, hslot=hslot, tc=tc:
                                a.activation(hslot, pA[:, tc * 512:(tc + 1) * 512],
                                             AF.Silu),
                                inc=AC, waits=wts)
                            p.v(lambda v, hslot=hslot, tc=tc:
                                v.tensor_mul(hslot, hslot,
                                             pA2[:, tc * 512:(tc + 1) * 512]),
                                inc=VE, waits=[(AC, AC.n)])
                            slot_ve[tc] = VE.n
                        else:
                            p.v(lambda v, tc=tc:
                                v.tensor_mul(vbuf, pA2[:, tc * 512:(tc + 1) * 512],
                                             zc[:, tc * 512:(tc + 1) * 512]),
                                waits=[(MM, mm_a), (zc_ch, zc_ld), (AC, AC.n)])
                            p.v(lambda v, tc=tc:
                                v.tensor_mul(ubuf, pA[:, tc * 512:(tc + 1) * 512],
                                             zc[:, tc * 512:(tc + 1) * 512]),
                                inc=VE)
                            slot_ve[tc] = VE.n
                            wts = [(VE, VE.n)]
                            if first_h[0]:
                                wts.append((LY, zout_ly))
                                first_h[0] = False
                            p.a(lambda a, hslot=hslot:
                                a.activation(hslot, ubuf, AF.Silu),
                                inc=AC, waits=wts)
                            p.v(lambda v, hslot=hslot:
                                v.tensor_mul(hslot, hslot, vbuf),
                                inc=VE, waits=[(AC, AC.n)])
                    slot_mm[i1s % 3] = MM.n
                    slot_mm[i3s % 3] = MM.n
                zc_ve_prev[0] = VE.n
                h_ve = VE.n
                if e == NE:
                    lnx_free_mm[0] = MM.n
                for do in range(NDO):
                    iw2 = w2idx[0]
                    w2idx[0] += 1
                    ch2 = w2_ch[iw2 % 2]
                    src2 = sw2H[do] if shared else w2H[e, do]
                    dmaW(ch2, w2s(iw2)[:, :], src2,
                         waits=[(MM, slot_mm.get(("w2", iw2 % 2), guard_mm))])
                    n2 = ch2.n
                    for tc in range(ntc):
                        pbank = pB if (do % 2 == 0) else pC
                        for j in range(NJ):
                            wts = []
                            if j == 0:
                                wts = [(ch2, n2),
                                       (VE, max(epi_ve.get((do % 2, tc), 0), h_ve))]
                                if do == 0 and tc == 0:
                                    wts.append((AC, AC.n))
                            p.t(lambda t, do=do, tc=tc, j=j, iw2=iw2, pbank=pbank:
                                t.matmul(pbank[:, tc * 512:(tc + 1) * 512],
                                         w2s(iw2)[:, j * 128:(j + 1) * 128],
                                         hsb[:, j * TOK + tc * 512:
                                             j * TOK + (tc + 1) * 512],
                                         start=(j == 0), stop=(j == NJ - 1)),
                                inc=MM, waits=wts)
                        p.v(lambda v, do=do, tc=tc, pbank=pbank:
                            v.scalar_tensor_tensor(
                                xr(do, tc * 512, (tc + 1) * 512),
                                pbank[:, tc * 512:(tc + 1) * 512], 0.5,
                                xr(do, tc * 512, (tc + 1) * 512),
                                ALU.mult, ALU.add),
                            inc=VE, waits=[(MM, MM.n)])
                        epi_ve[(do % 2, tc)] = VE.n
                        if do == 7:
                            spill_ve8[0] = VE.n
                        if do == NDO - 1:
                            spill_ve16[0] = VE.n
                    slot_mm[("w2", iw2 % 2)] = MM.n
            ffn_dbg[0] = False

        if PH >= -2:
            emit_load_x(x_peer)
        if PH >= 1:
            emit_ln(lambda k: lnx(k))
        if PH >= 2:
            emit_ffn(lambda k: lnx(k))
        if PH >= 3:
            emit_ln(lambda k: l2x(k, 0, TOK))

        if PH >= 4:
            emit_load_x(x_own)
            emit_ln(lambda k: lnx(k))
            emit_ffn(lambda k: lnx(k))
            emit_ln(lambda k: l2x(k, TOK, S))
        if PH >= 0:
            dmaY(xspill[:, 0:8 * TOK], bigA[:, 0:8 * TOK], waits=[(VE, spill_ve8[0])])
            dmaY(xspill[:, 8 * TOK:], bigA[:, 8 * TOK:], waits=[(VE, spill_ve16[0])])
        spill_ly = LY.n

        if 0 <= PH < 5:
            dmaY(bigA[:, :], xspill[:, :], waits=[(LY, spill_ly)])
        _full = PH >= 5
        BA = bigA.ap().bitcast(BF16)
        VOFF = 0
        PTOFF = 8448
        KOFF = 16640
        QOFF = 20736
        O1OFF = 21760
        RTOFF = 22784
        TBOFF = 23808
        QSOFF = 24320
        CCOFF = 27392
        CSOFF = 29440

        def Vslot(st, hh):
            return BA[:, VOFF + st * 528 + hh * 132: VOFF + st * 528 + hh * 132 + 132]

        def Vtile(st):
            return BA[:, VOFF + st * 528: VOFF + st * 528 + 528]

        def Pts(st):
            return BA[:, PTOFF + st * 512: PTOFF + (st + 1) * 512]

        def Kbuf(par):
            return BA[:, KOFF + par * 2048: KOFF + (par + 1) * 2048]

        Qb = BA[:, QOFF: QOFF + 1024]
        o1n = BA[:, O1OFF:O1OFF + 1024].bitcast(F32)
        rtmp = BA[:, RTOFF:RTOFF + 1024].bitcast(F32)
        tmpB = BA[:, TBOFF:TBOFF + 512]
        qsq = BA[:, QSOFF:QSOFF + 3072]
        cC = BA[:, CCOFF:CCOFF + 2048]
        cS2 = BA[:, CSOFF:CSOFF + 2048]

        ln2_ve = VE.n
        ln2_ac = AC.n
        if _full:
            p.v(lambda v: v.memset(BA[:, VOFF:VOFF + 8448], 1.0), inc=VE,
                waits=[(LY, spill_ly), (AC, ln2_ac)])
        vones_ve = VE.n
        if _full:
            dmaW(SL[7], cC[:, :], ropeC[:, :],
                 waits=[(LY, spill_ly), (VE, ln2_ve), (AC, ln2_ac)])
            dmaW(SL[7], cS2[:, :], ropeS2[:, :])
        rope_lw = SL[7].n

        wv_ch = SL[1]
        wk_ch = SL[2]
        wq_ch = SL[3]

        wvslot = wa[:, 0:8192]

        def wkslot(par):
            return hsb[:, par * 2048:(par + 1) * 2048]

        def wqslot(par):
            return hsb[:, 4096 + par * 2048: 4096 + (par + 1) * 2048]

        prevmm = {}
        pt_mm = {}
        pcg = {0: [(VE, vones_ve)], 1: [(VE, vones_ve)]}
        pbg = {0: [], 1: []}
        obg = {0: [], 1: []}
        sag = {0: [], 1: []}
        atb_mm = [0]

        for g4 in range(4 if _full else 0):
            dmaW(wv_ch, wvslot[:, :], wvH[g4],
                 waits=[(MM, prevmm.get("wv", MM.n))])
            nwv = wv_ch.n
            for st in range(16):
                vb = (st % 2) * 512
                for k in range(NK):
                    wts = []
                    if k == 0:
                        wts = [(wv_ch, nwv)] + pcg[st % 2] + (obg[1] if st < 2 else [])
                    p.t(lambda t, st=st, k=k, vb=vb:
                        t.matmul(pC[:, vb:vb + 512],
                                 l2x(k, st * 128, (st + 1) * 128),
                                 wvslot[:, k * 512:(k + 1) * 512],
                                 start=(k == 0), stop=(k == NK - 1)),
                        inc=MM, waits=wts)
                p.v(lambda v, st=st, vb=vb:
                    v.tensor_copy(
                        Vtile(st).rearrange("p (h c) -> p h c", c=132)[:, :, 0:128],
                        pC[:, vb:vb + 512].rearrange("p (h c) -> p h c", c=128)),
                    inc=VE, waits=[(MM, MM.n)])
                pcg[st % 2] = [(VE, VE.n)]
            v_ve = VE.n

            for hh in range(4):
                h = g4 * 4 + hh
                par = h % 2
                dmaW(wk_ch, wkslot(par)[:, :], wkH[h],
                     waits=[(MM, prevmm.get(("k", par), MM.n))])
                nk_ld = wk_ch.n
                dmaW(wq_ch, wqslot(par)[:, :], wqH[h],
                     waits=[(MM, prevmm.get(("q", par), MM.n))])
                nq_ld = wq_ch.n
                specs = [("k", 0), ("k", 1), ("k", 2), ("k", 3), ("q", 0), ("q", 1)]
                for idx, (kind, sc) in enumerate(specs):
                    pslice = pC[:, (idx % 2) * 512:((idx % 2) + 1) * 512]
                    if kind == "k":
                        dst = Kbuf(par)[:, sc * 512:(sc + 1) * 512]
                        abs0 = sc * 512
                        wch, nld = wk_ch, nk_ld

                        def wsl(k, par=par):
                            return wkslot(par)[:, k * 128:(k + 1) * 128]

                        def rhs(k, sc=sc):
                            return l2x(k, sc * 512, (sc + 1) * 512)
                    else:
                        dst = Qb[:, sc * 512:(sc + 1) * 512]
                        abs0 = 1024 + sc * 512
                        wch, nld = wq_ch, nq_ld

                        def wsl(k, par=par):
                            return wqslot(par)[:, k * 128:(k + 1) * 128]

                        def rhs(k, sc=sc):
                            return l2x(k, 1024 + sc * 512, 1024 + (sc + 1) * 512)
                    for k in range(NK):
                        wts = []
                        if k == 0:
                            wts = [(wch, nld)] + pcg[idx % 2]
                            if idx % 2 == 0:
                                wts += obg[1]
                        p.t(lambda t, k=k, pslice=pslice, wsl=wsl, rhs=rhs:
                            t.matmul(pslice, wsl(k), rhs(k),
                                     start=(k == 0), stop=(k == NK - 1)),
                            inc=MM, waits=wts)
                    mm_p = MM.n
                    p.a(lambda a, dst=dst, pslice=pslice:
                        a.copy(dst, pslice), inc=AC, waits=[(MM, mm_p)])
                    ac_raw = AC.n
                    p.t(lambda t, dst=dst, pslice=pslice:
                        t.matmul(pslice, permb[:, :], dst, start=True, stop=True),
                        inc=MM, waits=[(AC, ac_raw)])
                    mm_perm = MM.n
                    p.v(lambda v, dst=dst, abs0=abs0:
                        v.tensor_mul(tmpB[:, 0:512], dst, cC[:, abs0:abs0 + 512]),
                        waits=[(AC, ac_raw), (SL[7], rope_lw), (MM, atb_mm[0])])
                    p.v(lambda v, pslice=pslice, abs0=abs0:
                        v.tensor_mul(rtmp[:, 0:512], pslice, cS2[:, abs0:abs0 + 512]),
                        inc=VE, waits=[(MM, mm_perm)])
                    pcg[idx % 2] = [(VE, VE.n), (AC, ac_raw)]
                    p.v(lambda v, dst=dst:
                        v.tensor_add(dst, tmpB[:, 0:512], rtmp[:, 0:512]),
                        inc=VE)
                rope_ve = VE.n
                p.a(lambda a, par=par: a.activation(qsq[:, 0:2048], Kbuf(par), AF.Square),
                    inc=AC, waits=[(VE, rope_ve)])
                p.a(lambda a: a.activation(qsq[:, 2048:3072], Qb, AF.Square), inc=AC)
                sq_ac = AC.n
                km = sma[0:1, 512:513]
                k2 = sma[0:1, 513:514]
                qm = sma[0:1, 514:515]
                kmm = sma[0:1, 515:516]
                lnk = sma[0:1, 516:517]
                lnq = sma[0:1, 517:518]
                lsum = sma[0:1, 518:519]
                eB = sma[0:1, 519:520]
                nB = sma[0:1, 520:521]
                p.t(lambda t:
                    t.matmul(pB[0:1, 0:512], onesb[:, 0:1],
                             qsq[:, 0:512], start=True, stop=True),
                    inc=MM, waits=[(AC, sq_ac), (GP, GP.n)] + pbg[0] + pbg[1]
                    + obg[0])
                p.t(lambda t:
                    t.matmul(pB[0:1, 512:1024], onesb[:, 0:1],
                             qsq[:, 512:1024], start=True, stop=True),
                    inc=MM)
                p.v(lambda v: v.reduce_max(km, pB[0:1, 0:1024], AX.X),
                    inc=VE, waits=[(MM, MM.n)])
                p.t(lambda t:
                    t.matmul(pB[0:1, 0:512], onesb[:, 0:1],
                             qsq[:, 1024:1536], start=True, stop=True),
                    inc=MM, waits=[(VE, VE.n)])
                p.t(lambda t:
                    t.matmul(pB[0:1, 512:1024], onesb[:, 0:1],
                             qsq[:, 1536:2048], start=True, stop=True),
                    inc=MM)
                p.v(lambda v: v.reduce_max(k2, pB[0:1, 0:1024], AX.X),
                    inc=VE, waits=[(MM, MM.n)])
                p.g(lambda g: g.tensor_scalar_max(kmm, km, k2[0:1, 0:1]),
                    inc=GP, waits=[(VE, VE.n)])
                p.t(lambda t:
                    t.matmul(pB[0:1, 0:512], onesb[:, 0:1],
                             qsq[:, 2048:2560], start=True, stop=True),
                    inc=MM, waits=[(VE, VE.n)])
                p.t(lambda t:
                    t.matmul(pB[0:1, 512:1024], onesb[:, 0:1],
                             qsq[:, 2560:3072], start=True, stop=True),
                    inc=MM)
                p.v(lambda v: v.reduce_max(qm, pB[0:1, 0:1024], AX.X),
                    inc=VE, waits=[(MM, MM.n)])
                p.a(lambda a: a.activation(lnk, kmm, AF.Ln),
                    waits=[(GP, GP.n), (VE, VE.n)])
                p.a(lambda a: a.activation(lnq, qm, AF.Ln), inc=AC)
                p.g(lambda g: g.tensor_scalar_add(lsum, lnk, lnq[0:1, 0:1]),
                    inc=GP, waits=[(AC, AC.n)])
                p.a(lambda a: a.activation(eB, lsum, AF.Exp, scale=0.5,
                                           bias=cbias[0:1, 1:2]),
                    waits=[(GP, GP.n)])
                p.a(lambda a: a.activation(nB, eB, AF.Copy, scale=-1.0), inc=AC)
                p.a(lambda a: a.copy(lhl[0:1, 2:3], nB), inc=AC)
                p.t(lambda t: t.matmul(pB[:, 512:513], onesr[0:1, :],
                                       lhl[0:1, 2:3], start=True, stop=True),
                    inc=MM, waits=[(AC, AC.n), (VE, VE.n)])
                p.a(lambda a: a.copy(nBc[:, 0:1], pB[:, 512:513]),
                    inc=AC, waits=[(MM, MM.n)])
                p.a(lambda a: a.copy(nBc[:, 1:2], pB[:, 512:513]), inc=AC)
                pbg[0] = [(VE, VE.n)]
                pbg[1] = [(AC, AC.n)]

                atb4 = tmpB
                for qc in range(2):
                    for half in range(2):
                        r0 = half * 64
                        exp_acs = [0] * 16
                        for st in range(16):
                            pair = st // 2
                            psl = pA if (pair % 2 == 0) else pA2
                            col = (st % 2) * 512
                            wts = []
                            if st == 0 and half == 0 and qc == 0:
                                wts.append((VE, rope_ve))
                            if st % 2 == 0:
                                wts += sag[pair % 2]
                            p.t(lambda t, st=st, r0=r0, qc=qc, par=par, psl=psl,
                                col=col:
                                t.matmul(psl[:, col:col + 512],
                                         Kbuf(par)[r0:r0 + 64, st * 128:(st + 1) * 128],
                                         Qb[r0:r0 + 64, qc * 512:(qc + 1) * 512],
                                         start=True, stop=True),
                                inc=MM, waits=wts)
                            if st % 2 == 1:
                                mm_s = MM.n
                                wts2 = [(MM, mm_s)]
                                for stq in (st - 1, st):
                                    if ("pt", stq) in pt_mm:
                                        wts2.append((MM, pt_mm[("pt", stq)]))
                                p.a(lambda a, st=st, half=half, psl=psl:
                                    a.activation(Pts(st - 1).tensor.ap()
                                                 [:, PTOFF + (st - 1) * 512:
                                                  PTOFF + (st + 1) * 512]
                                                 if False else
                                                 BA[:, PTOFF + (st - 1) * 512:
                                                    PTOFF + (st + 1) * 512],
                                                 psl[:, 0:1024], AF.Exp,
                                                 bias=nBc[:, half:half + 1],
                                                 scale=0.125),
                                    inc=AC, waits=wts2)
                                exp_acs[st - 1] = AC.n
                                exp_acs[st] = AC.n
                                sag[pair % 2] = [(AC, AC.n)]
                        for qt in range(4):
                            ob = pB if (qt % 2 == 0) else pC
                            for st in range(16):
                                wts = []
                                if st == 0:
                                    wts += obg[qt % 2]
                                    wts += (pbg[0] if qt % 2 == 0
                                            else pcg[0] + pcg[1])
                                if qt == 0:
                                    wts.append((AC, exp_acs[st]))
                                p.t(lambda t, st=st, qt=qt, hh=hh, ob=ob:
                                    t.matmul(ob[:, 0:129],
                                             Pts(st)[:, qt * 128:(qt + 1) * 128],
                                             Vslot(st, hh)[:, 0:129],
                                             start=(st == 0), stop=(st == 15)),
                                    inc=MM, waits=wts)
                                pt_mm[("pt", st)] = MM.n
                            mm_pv = MM.n
                            if half == 0:
                                p.v(lambda v, ob=ob: v.reciprocal(
                                    zq[:, 0:1], ob[:, 128:129]),
                                    inc=VE, waits=[(MM, mm_pv)])
                                p.wait("v", VE)
                                p.v(lambda v, qt=qt, ob=ob: v.tensor_single_scalar(
                                    o1n[:, qt * 128:(qt + 1) * 128],
                                    ob[:, 0:128], zq[:, 0:1], ALU.mult), inc=VE)
                                obg[qt % 2] = [(VE, VE.n)]
                            else:
                                at0 = att[:, 0:128]
                                at1 = att[:, 128:256]
                                p.v(lambda v, ob=ob: v.reciprocal(
                                    zq[:, 1:2], ob[:, 128:129]),
                                    inc=VE, waits=[(MM, mm_pv)])
                                p.wait("v", VE)
                                p.v(lambda v: v.tensor_mul(zq[:, 2:3], zq[:, 1:2],
                                                           lamc[:, :]), inc=VE)
                                p.wait("v", VE)
                                p.v(lambda v, qt=qt, ob=ob, at0=at0:
                                    v.scalar_tensor_tensor(
                                        at0, ob[:, 0:128], zq[:, 2:3],
                                        o1n[:, qt * 128:(qt + 1) * 128],
                                        ALU.mult, ALU.add), inc=VE)
                                obg[qt % 2] = [(VE, VE.n)]
                                p.v(lambda v, at0=at0, at1=at1:
                                    v.tensor_mul(at1, at0, at0))
                                p.v(lambda v, at1=at1:
                                    v.reduce_sum(zq[:, 3:4], at1, AX.X), inc=VE)
                                p.a(lambda a: a.activation(zq[:, 3:4], zq[:, 3:4],
                                                           AF.Ln, scale=1.0 / 128,
                                                           bias=cbias[:, 0:1]),
                                    waits=[(VE, VE.n)])
                                p.a(lambda a: a.activation(zq[:, 3:4], zq[:, 3:4],
                                                           AF.Exp, scale=-0.5,
                                                           bias=cbias[:, 2:3]), inc=AC)
                                p.v(lambda v, at0=at0: v.tensor_single_scalar(
                                    at0, at0, zq[:, 3:4], ALU.mult),
                                    waits=[(AC, AC.n)])
                                w = [(MM, atb_mm[0])] if qt == 0 else []
                                p.v(lambda v, at0=at0, qt=qt:
                                    v.tensor_mul(atb4[:, qt * 128:(qt + 1) * 128],
                                                 at0, gamb[:, :]),
                                    inc=VE, waits=w)
                        if half == 1:
                            pass
                    att_ve2 = VE.n
                    cp_ve = {}
                    for qt in range(4):
                        tp = pA if qt % 2 == 0 else pA2
                        wts = [(VE, att_ve2), (AC, AC.n)] if qt == 0 else []
                        if qt >= 2:
                            wts = [(VE, cp_ve[qt - 2])]
                        p.t(lambda t, qt=qt, tp=tp:
                            t.transpose(tp[:, 0:64].bitcast(BF16)[:, 0:128],
                                        atb4[:, qt * 128:(qt + 1) * 128],
                                        identb[:, :]),
                            inc=MM, waits=wts)
                        p.v(lambda v, h=h, qc=qc, qt=qt, tp=tp:
                            v.tensor_copy(
                                mid[:, h * 1024 + qc * 512 + qt * 128:
                                    h * 1024 + qc * 512 + (qt + 1) * 128],
                                tp[:, 0:64].bitcast(BF16)[:, 0:128]),
                            inc=VE, waits=[(MM, MM.n)])
                        cp_ve[qt] = VE.n
                    atb_mm[0] = MM.n
                    sag[0] = sag[0] + [(VE, VE.n)]
                    sag[1] = sag[1] + [(VE, VE.n)]
                prevmm[("k", par)] = MM.n
                prevmm[("q", par)] = MM.n
            prevmm["wv"] = MM.n
        attn_ve = VE.n
        if os.environ.get("ATH_DBG") == "attn":
            dmaW(SL[0], dbg2.bitcast(BF16)[:, 0:NK * TOK], mid[:, :],
                 waits=[(VE, attn_ve)])
            dmaY(dbgd[299:300, 0:1], lamc[0:1, 0:1], waits=[(VE, attn_ve)])
            dmaY(dbgd[299:300, 1:2], lamc[64:65, 0:1], waits=[(VE, attn_ve)])
            dmaY(dbgd[298:299, 0:8], sma[0:1, 384:392], waits=[(VE, attn_ve)])
            dmaY(dbgd[297:298, 0:64], sma[0:1, 0:64], waits=[(VE, attn_ve)])
            dmaY(dbgd[296:297, 0:64], sma[0:1, 256:320], waits=[(VE, attn_ve)])

        if _full:
            dmaY(bigA[:, :], xspill[:, :],
                 waits=[(VE, attn_ve), (MM, MM.n), (AC, AC.n), (GP, GP.n)])
        reload_ly = LY.n
        wo_ch = SL[8]
        wo_prev = {}
        for fo in range(NDO if PH >= 6 else 0):
            sl = fo % 2

            def wos(sl=sl):
                return wa[:, sl * 2048:(sl + 1) * 2048]
            dmaW(wo_ch, wos()[:, :], woH[fo],
                 waits=[(MM, wo_prev.get(sl, MM.n))])
            nwo = wo_ch.n
            for qc in range(2):
                pbank = pB if (qc % 2 == 0) else pC
                for hh2 in range(NH):
                    wts = []
                    if hh2 == 0:
                        wts = [(wo_ch, nwo), (VE, VE.n)]
                    p.t(lambda t, qc=qc, hh2=hh2, pbank=pbank, wos=wos:
                        t.matmul(pbank[:, 0:512],
                                 wos()[:, hh2 * 128:(hh2 + 1) * 128],
                                 mid[:, hh2 * 1024 + qc * 512:
                                     hh2 * 1024 + (qc + 1) * 512],
                                 start=(hh2 == 0), stop=(hh2 == NH - 1)),
                        inc=MM, waits=wts)
                p.v(lambda v, fo=fo, qc=qc, pbank=pbank:
                    v.tensor_add(xr(fo, qc * 512, (qc + 1) * 512),
                                 pbank[:, 0:512],
                                 xr(fo, qc * 512, (qc + 1) * 512)),
                    inc=VE, waits=[(MM, MM.n), (LY, reload_ly)])
            wo_prev[sl] = MM.n

        if PH >= 7:
            emit_ln(lambda k: lnx(k))
            emit_ffn(lambda k: lnx(k))

        midf2 = mid.ap().bitcast(F32)

        def stage(par):
            return midf2[:, par * 2048:(par + 1) * 2048]
        st_ch = SL[9]
        mm_out0 = MM.n
        ve_out0 = VE.n
        cp_cnt = {}
        tslots = [pC[:, 0:128], pC[:, 512:640], pB[:, 0:128], pB[:, 512:640]]
        for tt in range(8):
            par = tt % 2
            for k in range(NK):
                i = tt * NK + k
                pk = i % 4
                if i < 4:
                    wts = [(VE, ve_out0)]
                    if i == 0:
                        wts.append((MM, mm_out0))
                        wts.append((LY, LY.n))
                else:
                    wts = [(VE, cp_cnt[i - 4])]
                if tt >= 2 and k == 0:
                    wts.append((st_ch, st_ch.n - 16))
                p.t(lambda t, tt=tt, k=k, pk=pk:
                    t.transpose(tslots[pk],
                                xr(k, tt * 128, (tt + 1) * 128), identf[:, :]),
                    inc=MM, waits=wts)
                p.v(lambda v, tt=tt, k=k, pk=pk, par=par:
                    v.tensor_copy(stage(par)[:, k * 128:(k + 1) * 128],
                                  tslots[pk]),
                    inc=VE, waits=[(MM, MM.n)])
                cp_cnt[i] = VE.n
            dmaY(out_d[tt * 128:(tt + 1) * 128, :], stage(par)[:, :],
                 waits=[(VE, VE.n)], ch=st_ch)
        p.wait("y", st_ch)
        p.wait("g", st_ch)

        p.check()

        with nc.Block() as block:
            block.tensor(lambda te: ([f(te) for f in p.q["t"]], None)[1])
            block.vector(lambda ve: ([f(ve) for f in p.q["v"]], None)[1])
            block.scalar(lambda ac: ([f(ac) for f in p.q["a"]], None)[1])
            block.gpsimd(lambda gp: ([f(gp) for f in p.q["g"]], None)[1])
            block.sync(lambda sy: ([f(sy) for f in p.q["y"]], None)[1])

    return nc


def _host_layouts(inputs):
    Wq = np.asarray(inputs["L0_attn_Wq"])
    Wk = np.asarray(inputs["L0_attn_Wk"])
    Wv = np.asarray(inputs["L0_attn_Wv"])
    Wo = np.asarray(inputs["L0_attn_Wo"])
    w1 = np.asarray(inputs["L0_ffn_w1"])
    w2 = np.asarray(inputs["L0_ffn_w2"])
    w3 = np.asarray(inputs["L0_ffn_w3"])
    sw1 = np.asarray(inputs["L0_ffn_sw1"])
    sw2 = np.asarray(inputs["L0_ffn_sw2"])
    sw3 = np.asarray(inputs["L0_ffn_sw3"])
    gw = np.asarray(inputs["L0_ffn_gate_w"])
    C = np.ascontiguousarray
    d = {}
    d["wkH"] = C(Wk.reshape(16, 128, 16, 128).transpose(0, 3, 2, 1).reshape(16, 128, 2048))
    d["wqH"] = C(Wq.reshape(16, 128, 16, 128).transpose(0, 3, 2, 1).reshape(16, 128, 2048))
    d["wvH"] = C(Wv.reshape(4, 512, 16, 128).transpose(0, 3, 2, 1).reshape(4, 128, 8192))
    d["woH"] = C(Wo.reshape(16, 128, 16, 128).transpose(0, 3, 2, 1).reshape(16, 128, 2048))
    d["gateH"] = C(gw.T.reshape(16, 128, 8).transpose(1, 0, 2).reshape(128, 128))
    d["w1H"] = C(w1.reshape(8, 8, 128, 16, 128).transpose(0, 1, 4, 3, 2)
                 .reshape(8, 8, 128, 2048))
    d["w3H"] = C(w3.reshape(8, 8, 128, 16, 128).transpose(0, 1, 4, 3, 2)
                 .reshape(8, 8, 128, 2048))
    d["w2H"] = C(w2.reshape(8, 16, 128, 8, 128).transpose(0, 1, 4, 3, 2)
                 .reshape(8, 16, 128, 1024))
    d["sw1H"] = C(sw1.reshape(8, 128, 16, 128).transpose(0, 3, 2, 1).reshape(8, 128, 2048))
    d["sw3H"] = C(sw3.reshape(8, 128, 16, 128).transpose(0, 3, 2, 1).reshape(8, 128, 2048))
    d["sw2H"] = C(sw2.reshape(16, 128, 8, 128).transpose(0, 3, 2, 1).reshape(16, 128, 1024))
    d["gamma"] = C(np.asarray(inputs["L0_attn_gamma"]))
    d["ebias"] = C(np.asarray(inputs["L0_ffn_expert_bias"]))
    d["lq1"] = C(np.asarray(inputs["L0_attn_lambda_q1"]))
    d["lk1"] = C(np.asarray(inputs["L0_attn_lambda_k1"]))
    d["lq2"] = C(np.asarray(inputs["L0_attn_lambda_q2"]))
    d["lk2"] = C(np.asarray(inputs["L0_attn_lambda_k2"]))
    d["identf"] = np.eye(128, dtype=np.float32)
    d["identb"] = np.eye(128, dtype=BF)
    permv = np.zeros((128, 128), dtype=BF)
    for i in range(128):
        permv[i ^ 1, i] = 1.0
    d["permb"] = permv
    d["iota1000"] = np.broadcast_to((1000.0 + np.arange(8)).astype(np.float32),
                                    (128, 8)).copy()
    return d


def _rope_tables(positions):
    HDh = 32
    angle = 1.0 / 10000 ** np.linspace(0.0, 1.0, HDh, dtype=np.float32)
    freqs = positions[None, :].astype(np.float32) * angle[:, None]
    cos = np.cos(freqs).astype(np.float32)
    sin = np.sin(freqs).astype(np.float32)
    Cp = np.empty((128, 2048), np.float32)
    S2 = np.empty((128, 2048), np.float32)
    for pp in range(128):
        i = (pp % 64) // 2
        Cp[pp] = cos[i]
        S2[pp] = -sin[i] if (pp % 2 == 0) else sin[i]
    return Cp, S2


def kernel(**inputs):
    x = np.asarray(inputs["L0_attn_x"])
    shared = _host_layouts(inputs)
    nc = build_program()
    in_maps = []
    for c in range(8):
        b, hf = c // 2, c % 2
        xo = np.ascontiguousarray(x[b, hf * 1024:(hf + 1) * 1024])
        xp = np.ascontiguousarray(x[b, (1 - hf) * 1024:(2 - hf) * 1024])
        peer_pos = np.arange((1 - hf) * 1024, (2 - hf) * 1024)
        own_pos = np.arange(hf * 1024, (hf + 1) * 1024)
        Cp, S2 = _rope_tables(np.concatenate([peer_pos, own_pos]))
        m = dict(shared)
        m["x_own"] = xo
        m["x_peer"] = xp
        m["ropeC"] = Cp
        m["ropeS2"] = S2
        in_maps.append(m)
    res = run_bass_kernel_spmd(nc, in_maps, core_ids=list(range(8)))
    out = np.empty((4, 2048, 2048), np.float32)
    for c, r in enumerate(res.results):
        b, hf = c // 2, c % 2
        out[b, hf * 1024:(hf + 1) * 1024] = r["out"]
    return out


# version 07
# speedup vs baseline: 1.0776x; 1.0249x over previous
import math
import os
import numpy as np
import ml_dtypes
import concourse.bass as bass
import concourse.mybir as mybir
from concourse.bass_utils import run_bass_kernel_spmd
from contextlib import ExitStack

F32 = mybir.dt.float32
BF16 = mybir.dt.bfloat16
AF = mybir.ActivationFunctionType
ALU = mybir.AluOpType
AX = mybir.AxisListType
BF = ml_dtypes.bfloat16

D = 2048
TOK = 1024
S = 2048
NH = 16
NE = 8
HID = 1024
NK = 16
NJ = 8
NDO = 16
LAMB_INIT = 0.8 - 0.6 * math.exp(-0.3 * 12)
EPS = 1e-5
LN8 = math.log(8.0)
LN_OMLI = math.log(1.0 - LAMB_INIT)


class Ch:
    def __init__(self, h, step=1):
        self.h = h
        self.n = 0
        self.step = step


class Prog:
    def __init__(self):
        self.q = {e: [] for e in "tvagy"}
        self.waits = []

    def _emit(self, eng, fn, inc, waits):
        for (c, n) in waits:
            if n > 0:
                self.waits.append((c, n))
        step_amt = None
        if inc is not None:
            inc.n += inc.step
            step_amt = inc.step

        def run(e, fn=fn, inc=inc, waits=tuple(waits), step_amt=step_amt):
            for (c, n) in waits:
                if n > 0:
                    e.wait_ge(c.h, n)
            ins = fn(e)
            if inc is not None:
                ins.then_inc(inc.h, step_amt)
        self.q[eng].append(run)

    def t(self, fn, inc=None, waits=()):
        self._emit("t", fn, inc, waits)

    def v(self, fn, inc=None, waits=()):
        self._emit("v", fn, inc, waits)

    def a(self, fn, inc=None, waits=()):
        self._emit("a", fn, inc, waits)

    def g(self, fn, inc=None, waits=()):
        self._emit("g", fn, inc, waits)

    def y(self, fn, inc=None, waits=()):
        self._emit("y", fn, inc, waits)

    def wait(self, eng, ch, n=None):
        n = ch.n if n is None else n
        if n <= 0:
            return
        self.waits.append((ch, n))

        def run(e, ch=ch, n=n):
            e.wait_ge(ch.h, n)
        self.q[eng].append(run)

    def check(self):
        for (c, n) in self.waits:
            assert n <= c.n, f"wait {n} > final {c.n}"


def build_program():
    nc = bass.Bass("TRN2", target_bir_lowering=False)

    def din(name, shape, dt=F32):
        return nc.dram_tensor(name, shape, dt, kind="ExternalInput").ap()

    x_own = din("x_own", [TOK, D])
    x_peer = din("x_peer", [TOK, D])
    ropeC = din("ropeC", [128, S])
    ropeS2 = din("ropeS2", [128, S])
    wkH = din("wkH", [NH, 128, NK * 128])
    wqH = din("wqH", [NH, 128, NK * 128])
    wvH = din("wvH", [4, 128, NK * 512])
    woH = din("woH", [NDO, 128, NH * 128])
    gateH = din("gateH", [128, 128])
    w1H = din("w1H", [NE, NJ, 128, NK * 128])
    w3H = din("w3H", [NE, NJ, 128, NK * 128])
    w2H = din("w2H", [NE, NDO, 128, NJ * 128])
    sw1H = din("sw1H", [NJ, 128, NK * 128])
    sw3H = din("sw3H", [NJ, 128, NK * 128])
    sw2H = din("sw2H", [NDO, 128, NJ * 128])
    gamma = din("gamma", [128])
    ebias = din("ebias", [8])
    lq1 = din("lq1", [64])
    lk1 = din("lk1", [64])
    lq2 = din("lq2", [64])
    lk2 = din("lk2", [64])
    identf_d = din("identf", [128, 128])
    identb_d = din("identb", [128, 128], BF16)
    permb_d = din("permb", [128, 128], BF16)
    iota_d = din("iota1000", [128, 8])

    out_d = nc.dram_tensor("out", [TOK, D], F32, kind="ExternalOutput").ap()
    xspill = nc.dram_tensor("xspill", [128, NK * TOK], F32, kind="Internal").ap()
    dscr = nc.dram_tensor("dscr", [8192], F32, kind="Internal").ap()
    zscd = nc.dram_tensor("zscd", [8, TOK], F32, kind="Internal").ap()
    dbgd = nc.dram_tensor("dbgd", [512, 2048], F32, kind="Internal").ap()
    dbg2 = nc.dram_tensor("dbg2", [128, NK * TOK], F32, kind="Internal").ap()

    def drow(c0, c1):
        return dscr[c0:c1].rearrange("(o j) -> o j", o=1)

    with ExitStack() as ctx:
        E = ctx.enter_context
        bigA = E(nc.sbuf_tensor("bigA", [128, NK * TOK], F32))
        ln2x = E(nc.sbuf_tensor("ln2x", [128, NK * S], BF16))
        mid = E(nc.sbuf_tensor("mid", [128, NK * TOK], BF16))
        wa = E(nc.sbuf_tensor("wa", [128, 8192], BF16))
        hsb = E(nc.sbuf_tensor("hsb", [128, NJ * TOK], BF16))
        sma = E(nc.sbuf_tensor("sma", [128, 2048], F32))
        identf = E(nc.sbuf_tensor("identf_s", [128, 128], F32))
        identb = E(nc.sbuf_tensor("identb_s", [128, 128], BF16))
        permb = E(nc.sbuf_tensor("permb_s", [128, 128], BF16))
        gamb = E(nc.sbuf_tensor("gamb", [128, 128], F32))
        iot = E(nc.sbuf_tensor("iot", [128, 8], F32))
        ebb = E(nc.sbuf_tensor("ebb", [128, 8], F32))
        onesb = E(nc.sbuf_tensor("onesb", [128, 1], BF16))
        gateS = E(nc.sbuf_tensor("gateS", [128, 128], BF16))
        lamc = E(nc.sbuf_tensor("lamc", [128, 1], F32))
        nBc = E(nc.sbuf_tensor("nBc", [128, 2], F32))
        zq = E(nc.sbuf_tensor("zq", [128, 4], F32))
        att = E(nc.sbuf_tensor("att", [128, 384], F32))
        cbias = E(nc.sbuf_tensor("cbias", [128, 4], F32))
        onesr = E(nc.sbuf_tensor("onesr", [1, 128], BF16))
        lhl = E(nc.sbuf_tensor("lhl", [1, 4], BF16))

        pA = E(nc.psum_tensor("pA", [128, 1024], F32))
        pA2 = E(nc.psum_tensor("pA2", [128, 1024], F32))
        pB = E(nc.psum_tensor("pB", [128, 1024], F32))
        pC = E(nc.psum_tensor("pC", [128, 1024], F32))

        sem_names = ["LW", "LY", "MM", "VE", "AC", "GP",
                     "s0", "s1", "s2", "s3", "s4", "s5", "s6", "s7", "s8", "s9",
                     "s10", "s11", "s12"]
        hs = {n: E(nc.semaphore(n)) for n in sem_names}
        LW = Ch(hs["LW"], 16)
        LY = Ch(hs["LY"], 16)
        MM = Ch(hs["MM"], 1)
        VE = Ch(hs["VE"], 1)
        AC = Ch(hs["AC"], 1)
        GP = Ch(hs["GP"], 1)
        SL = [Ch(hs[f"s{i}"], 16) for i in range(13)]

        p = Prog()

        def dmaW(ch, out_ap, in_ap, waits=()):
            ch.n += 16
            n = ch.n

            def run(g, o=out_ap, i=in_ap, waits=tuple(waits), ch=ch):
                for (c, w) in waits:
                    if w > 0:
                        g.wait_ge(c.h, w)
                g.dma_start(o, i).then_inc(ch.h, 16)
            for (c, w) in waits:
                if w > 0:
                    p.waits.append((c, w))
            p.q["g"].append(run)

        def dmaY(out_ap, in_ap, waits=(), ch=None):
            ch = LY if ch is None else ch
            ch.n += 16
            n = ch.n

            def run(y, o=out_ap, i=in_ap, waits=tuple(waits), ch=ch):
                for (c, w) in waits:
                    if w > 0:
                        y.wait_ge(c.h, w)
                y.dma_start(o, i).then_inc(ch.h, 16)
            for (c, w) in waits:
                if w > 0:
                    p.waits.append((c, w))
            p.q["y"].append(run)

        def xr(k, c0=0, c1=TOK):
            return bigA[:, k * TOK + c0: k * TOK + c1]

        def lnx(k):
            return mid[:, k * TOK:(k + 1) * TOK]

        def l2x(k, c0=0, c1=S):
            return ln2x[:, k * S + c0: k * S + c1]

        PH = int(os.environ.get("ATH_PH", "9"))
        STARTUP = PH >= -1
        dmaY(identf[:, :], identf_d[:, :])
        dmaY(identb[:, :], identb_d[:, :])
        dmaY(permb[:, :], permb_d[:, :])
        dmaY(iot[:, :], iota_d[:, :])
        if STARTUP:
            dmaY(gamb[:, :], gamma.rearrange("(o j) -> o j", o=1).broadcast_to([128, 128]))
            dmaY(ebb[:, :], ebias.rearrange("(o j) -> o j", o=1).broadcast_to([128, 8]))
            dmaY(sma[0:1, 0:64], lq1.rearrange("(o j) -> o j", o=1))
            dmaY(sma[0:1, 64:128], lk1.rearrange("(o j) -> o j", o=1))
            dmaY(sma[0:1, 128:192], lq2.rearrange("(o j) -> o j", o=1))
            dmaY(sma[0:1, 192:256], lk2.rearrange("(o j) -> o j", o=1))
        dmaW(LW, gateS[:, :], gateH[:, :])
        gate_lw = LW.n
        p.v(lambda v: v.memset(onesb[:, :], 1.0), inc=VE)
        p.v(lambda v: v.memset(onesr[:, :], 1.0), inc=VE)
        p.v(lambda v: v.memset(cbias[:, 0:1], EPS), inc=VE)
        p.v(lambda v: v.memset(cbias[:, 1:2], -LN8), inc=VE)
        p.v(lambda v: v.memset(cbias[:, 2:3], LN_OMLI), inc=VE)

        consts_ly = LY.n
        if not STARTUP:
            p.v(lambda v: v.memset(sma[0:1, 0:400], 0.5), inc=VE, waits=[(LY, consts_ly)])
        if STARTUP:
            p.v(lambda v: v.tensor_mul(sma[0:1, 256:320], sma[0:1, 0:64], sma[0:1, 64:128]),
                inc=VE, waits=[(LY, consts_ly)])
            p.v(lambda v: v.tensor_mul(sma[0:1, 320:384], sma[0:1, 128:192], sma[0:1, 192:256]),
                inc=VE)
            p.wait("v", VE)
            p.v(lambda v: v.reduce_sum(sma[0:1, 384:385], sma[0:1, 256:320], AX.X),
                inc=VE)
            p.v(lambda v: v.reduce_sum(sma[0:1, 385:386], sma[0:1, 320:384], AX.X), inc=VE)
            p.a(lambda a: a.activation(sma[0:1, 386:387], sma[0:1, 384:385], AF.Exp),
                waits=[(VE, VE.n)])
            p.a(lambda a: a.activation(sma[0:1, 387:388], sma[0:1, 385:386], AF.Exp), inc=AC)
            p.v(lambda v: v.tensor_sub(sma[0:1, 388:389], sma[0:1, 387:388], sma[0:1, 386:387]),
                inc=VE, waits=[(AC, AC.n)])
            p.wait("v", VE)
            p.v(lambda v: v.tensor_scalar_add(sma[0:1, 389:390], sma[0:1, 388:389], -LAMB_INIT),
                inc=VE)
            p.a(lambda a: a.copy(lhl[0:1, 0:1], sma[0:1, 389:390]),
                inc=AC, waits=[(VE, VE.n)])
            p.v(lambda v: v.tensor_sub(sma[0:1, 390:391], sma[0:1, 389:390],
                                       lhl[0:1, 0:1]),
                inc=VE, waits=[(AC, AC.n)])
            p.a(lambda a: a.copy(lhl[0:1, 1:2], sma[0:1, 390:391]),
                inc=AC, waits=[(VE, VE.n)])
            p.t(lambda t: t.matmul(pB[:, 0:1], onesr[0:1, :], lhl[0:1, 0:1],
                                   start=True, stop=False),
                waits=[(AC, AC.n), (VE, VE.n)])
            p.t(lambda t: t.matmul(pB[:, 0:1], onesr[0:1, :], lhl[0:1, 1:2],
                                   start=False, stop=True), inc=MM)
            p.a(lambda a: a.copy(lamc[:, :], pB[:, 0:1]),
                inc=AC, waits=[(MM, MM.n)])

        def emit_ln(dst_fn):
            Tn = TOK
            hb = hsb

            def xb_slot(par):
                return hb[:, par * Tn:(par + 1) * Tn]

            def xsq_slot(par):
                return hb[:, (2 + par) * Tn:(3 + par) * Tn]
            mm0 = MM.n
            ve_in0 = VE.n
            guard_mm = MM.n
            for k in range(NK):
                par = k % 2
                w = []
                if k < 2:
                    w = [(VE, ve_in0), (MM, guard_mm)]
                else:
                    w = [(MM, mm0 + 4 * (k - 1))]
                p.a(lambda a, k=k, par=par: a.copy(xb_slot(par), xr(k)),
                    inc=AC, waits=w)
                p.a(lambda a, k=k, par=par:
                    a.activation(xsq_slot(par), xr(k), AF.Square), inc=AC)
                for tc in range(2):
                    c0, c1 = tc * 512, (tc + 1) * 512
                    wm = [(AC, AC.n)]
                    if k == 0 and tc == 0:
                        wm.append((VE, VE.n))
                        wm.append((AC, AC.n))
                    p.t(lambda t, par=par, c0=c0, c1=c1, k=k:
                        t.matmul(pB[0:1, c0:c1], onesb[:, :], xb_slot(par)[:, c0:c1],
                                 start=(k == 0), stop=(k == NK - 1)),
                        inc=MM, waits=wm)
                    p.t(lambda t, par=par, c0=c0, c1=c1, k=k:
                        t.matmul(pA[0:1, c0:c1], onesb[:, :], xsq_slot(par)[:, c0:c1],
                                 start=(k == 0), stop=(k == NK - 1)),
                        inc=MM)
            mrow = sma[0:1, 1024:1024 + Tn]
            vrow = sma[0:1, 0:Tn]
            hb16 = hsb.ap()
            m_bf = hb16[0:1, 4096:4096 + Tn]
            rhi = hb16[0:1, 5120:5120 + Tn]
            rlo = hb16[0:1, 6144:6144 + Tn]
            p.a(lambda a: a.activation(mrow, pB[0:1, 0:Tn], AF.Copy, scale=1.0 / D),
                inc=AC, waits=[(MM, MM.n)])
            p.a(lambda a: a.copy(m_bf, mrow), inc=AC)
            p.v(lambda v: v.tensor_mul(vrow, mrow, mrow), waits=[(AC, AC.n)])
            p.v(lambda v: v.scalar_tensor_tensor(vrow, pA[0:1, 0:Tn], 1.0 / D,
                                                 vrow, ALU.mult, ALU.subtract), inc=VE)
            p.a(lambda a: a.activation(mrow, vrow, AF.Ln, bias=cbias[0:1, 0:1]),
                waits=[(VE, VE.n)])
            p.a(lambda a: a.activation(mrow, mrow, AF.Exp, scale=-0.5), inc=AC)
            p.a(lambda a: a.copy(rhi, mrow), inc=AC)
            p.v(lambda v: v.tensor_sub(vrow, mrow, rhi), inc=VE, waits=[(AC, AC.n)])
            p.a(lambda a: a.copy(rlo, vrow), inc=AC, waits=[(VE, VE.n)])
            mm_bc0 = MM.n
            for cc in range(2):
                p.t(lambda t, cc=cc:
                    t.matmul(pB[:, cc * 512:(cc + 1) * 512], onesr[0:1, :],
                             m_bf[0:1, cc * 512:(cc + 1) * 512],
                             start=True, stop=True),
                    inc=MM, waits=[(AC, AC.n), (VE, VE.n)] if cc == 0 else [])
                p.t(lambda t, cc=cc:
                    t.matmul(pA[:, cc * 512:(cc + 1) * 512], onesr[0:1, :],
                             rhi[0:1, cc * 512:(cc + 1) * 512],
                             start=True, stop=False),
                    inc=MM)
                p.t(lambda t, cc=cc:
                    t.matmul(pA[:, cc * 512:(cc + 1) * 512], onesr[0:1, :],
                             rlo[0:1, cc * 512:(cc + 1) * 512],
                             start=False, stop=True),
                    inc=MM)
            mm_bc = MM.n
            for k in range(NK):
                w = [(MM, mm_bc)] if k == 0 else []
                p.v(lambda v, k=k: v.tensor_sub(dst_fn(k), xr(k), pB[:, 0:Tn]),
                    inc=VE, waits=w)
            for k in range(NK):
                p.v(lambda v, k=k: v.tensor_mul(dst_fn(k), dst_fn(k), pA[:, 0:Tn]),
                    inc=VE)

        def emit_load_x(src_dram):
            midf = mid.ap().bitcast(F32)

            def xslot(par):
                return midf[:, par * 2048:(par + 1) * 2048]
            ly0 = LY.n
            mm0 = MM.n
            ve0 = VE.n
            guard = max(lnx_free_mm[0], 1) if lnx_free_mm[0] else MM.n
            for tt in range(8):
                par = tt % 2
                w = [(MM, mm0 + 16 * (tt - 1))] if tt >= 2 else [(MM, guard)]
                dmaY(xslot(par), src_dram[tt * 128:(tt + 1) * 128, :], waits=w)
                for k in range(NK):
                    pk = k % 2
                    i = 16 * tt + k
                    wv = []
                    if k == 0:
                        wv.append((LY, ly0 + 16 * (tt + 1)))
                    if i >= 2:
                        wv.append((VE, ve0 + i - 1))
                    else:
                        wv.append((VE, ve0))
                    p.t(lambda t, tt=tt, k=k, pk=pk, par=par:
                        t.transpose(pC[:, pk * 512:pk * 512 + 128],
                                    xslot(par)[:, k * 128:(k + 1) * 128], identf[:, :]),
                        inc=MM, waits=wv)
                    p.v(lambda v, tt=tt, k=k, pk=pk:
                        v.tensor_copy(xr(k, tt * 128, (tt + 1) * 128),
                                      pC[:, pk * 512:pk * 512 + 128]),
                        inc=VE, waits=[(MM, MM.n)])

        ffn_dbg = [True]
        spill_ve8 = [0]
        spill_ve16 = [0]
        lnx_free_mm = [0]

        def emit_ffn(lnsrc):
            Tn = TOK
            ntc = 2
            guard_mm = MM.n
            for k in range(NK):
                for tc in range(ntc):
                    w = []
                    if k == 0 and tc == 0:
                        w = [(LW, gate_lw), (VE, VE.n), (AC, AC.n)]
                    p.t(lambda t, k=k, tc=tc:
                        t.matmul(pB[0:8, tc * 512:(tc + 1) * 512],
                                 gateS[:, k * 8:(k + 1) * 8],
                                 lnsrc(k)[:, tc * 512:(tc + 1) * 512],
                                 start=(k == 0), stop=(k == NK - 1)),
                        inc=MM, waits=w)
            p.a(lambda a: a.activation(sma[0:8, 0:Tn], pB[0:8, 0:Tn], AF.Sigmoid),
                inc=AC, waits=[(MM, MM.n)])
            if os.environ.get("ATH_DBG") == "gate" and ffn_dbg[0]:
                dmaY(dbgd[0:8, 0:Tn], sma[0:8, 0:Tn], waits=[(AC, AC.n)])
            zfm = hsb.ap().bitcast(F32)
            bt = sma[:, 1024:1032]
            mx = sma[:, 1032:1040]
            ug = sma[:, 1040:1048]
            sg = sma[:, 1048:1056]
            t1g = sma[:, 1056:1064]
            scl = sma[:, 1064:1072]
            selg = sma[:, 1072:1080]
            zsg = sma[:, 1080:1088]
            for tt in range(Tn // 128):
                w = [(AC, AC.n)]
                if tt > 0:
                    w.append((VE, VE.n))
                p.t(lambda t, tt=tt:
                    t.transpose(pA[:, 0:8],
                                sma[0:8, tt * 128:(tt + 1) * 128], identf[0:8, 0:8]),
                    inc=MM, waits=w)
                mmt = MM.n
                p.v(lambda v: v.tensor_add(bt, pA[:, 0:8], ebb[:, :]),
                    inc=VE, waits=[(MM, mmt), (GP, GP.n)])
                p.wait("v", VE)
                p.v(lambda v: v.max(mx, bt), inc=VE)
                p.g(lambda g: g.tensor_single_scalar(ug, bt, mx[:, 1:2], ALU.subtract),
                    inc=GP, waits=[(VE, VE.n)])
                p.a(lambda a: a.activation(sg, ug, AF.Sign),
                    inc=AC, waits=[(GP, GP.n)])
                p.v(lambda v: v.tensor_scalar_add(t1g, sg, 1.0),
                    waits=[(AC, AC.n)])
                p.v(lambda v: v.tensor_scalar_max(scl, sg, 0.0), inc=VE)
                p.wait("v", VE)
                p.v(lambda v: v.tensor_sub(selg, t1g, scl), inc=VE)
                p.wait("v", VE)
                p.v(lambda v: v.tensor_mul(zsg, selg, pA[:, 0:8]), inc=VE)
                p.t(lambda t, tt=tt:
                    t.transpose(pA2[0:8, 0:128], zsg, identf[:, :]),
                    inc=MM, waits=[(VE, VE.n)])
                p.a(lambda a, tt=tt: a.copy(zfm[0:8, tt * 128:(tt + 1) * 128],
                                            pA2[0:8, 0:128]),
                    inc=AC, waits=[(MM, MM.n)])
            dmaY(zscd[:, 0:Tn], zfm[0:8, 0:Tn], waits=[(AC, AC.n)])
            zout_ly = LY.n

            def w13s(i):
                return wa[:, (i % 3) * 2048:((i % 3) + 1) * 2048]

            def w2s(i):
                return wa[:, 6144 + (i % 2) * 1024: 6144 + ((i % 2) + 1) * 1024]
            zc = sma[:, 0:Tn]
            ubuf = sma[:, 1024:1536]
            vbuf = sma[:, 1536:2048]
            w13_ch = [SL[1], SL[2], SL[3]]
            w13b_ch = [SL[10], SL[11], SL[12]]
            w2_ch = [SL[4], SL[5]]
            zc_ch = SL[6]
            gidx = [0]
            w2idx = [0]
            slot_mm = {}
            zc_ve_prev = [VE.n]
            first_h = [True]
            slot_ve = {0: VE.n, 1: VE.n}
            epi_ve = {}

            for e in range(NE + 1):
                shared = (e == NE)
                if not shared:
                    dmaY(zc, zscd[e:e + 1, 0:Tn].broadcast_to([128, Tn]),
                         waits=[(LY, zout_ly), (VE, zc_ve_prev[0])], ch=zc_ch)
                zc_ld = zc_ch.n
                for j in range(NJ):
                    i1s = gidx[0]
                    i3s = gidx[0] + 1
                    gidx[0] += 2
                    src1 = sw1H[j] if shared else w1H[e, j]
                    src3 = sw3H[j] if shared else w3H[e, j]
                    ch1 = w13_ch[i1s % 3]
                    ch1b = w13b_ch[i1s % 3]
                    ch3 = w13_ch[i3s % 3]
                    ch3b = w13b_ch[i3s % 3]
                    gw1 = [(MM, slot_mm.get(i1s % 3, guard_mm))]
                    dmaW(ch1, w13s(i1s)[:, 0:1024], src1[:, 0:1024], waits=gw1)
                    n1 = ch1.n
                    dmaW(ch1b, w13s(i1s)[:, 1024:2048], src1[:, 1024:2048])
                    n1b = ch1b.n
                    gw3 = [(MM, slot_mm.get(i3s % 3, guard_mm))]
                    dmaW(ch3, w13s(i3s)[:, 0:1024], src3[:, 0:1024], waits=gw3)
                    n3 = ch3.n
                    dmaW(ch3b, w13s(i3s)[:, 1024:2048], src3[:, 1024:2048])
                    n3b = ch3b.n
                    for tc in range(ntc):
                        for k in range(NK):
                            wts = []
                            if k == 0:
                                wts = [(ch1, n1), (VE, slot_ve[tc])]
                            if k == 8 and tc == 0:
                                wts = [(ch1b, n1b)]
                            p.t(lambda t, tc=tc, k=k, i1s=i1s:
                                t.matmul(pA[:, tc * 512:(tc + 1) * 512],
                                         w13s(i1s)[:, k * 128:(k + 1) * 128],
                                         lnsrc(k)[:, tc * 512:(tc + 1) * 512],
                                         start=(k == 0), stop=(k == NK - 1)),
                                inc=MM, waits=wts)
                        for k in range(NK):
                            wts = [(ch3, n3)] if k == 0 and tc == 0 else []
                            if k == 8 and tc == 0:
                                wts = [(ch3b, n3b)]
                            p.t(lambda t, tc=tc, k=k, i3s=i3s:
                                t.matmul(pA2[:, tc * 512:(tc + 1) * 512],
                                         w13s(i3s)[:, k * 128:(k + 1) * 128],
                                         lnsrc(k)[:, tc * 512:(tc + 1) * 512],
                                         start=(k == 0), stop=(k == NK - 1)),
                                inc=MM, waits=wts)
                        mm_a = MM.n
                        hslot = hsb[:, j * TOK + tc * 512: j * TOK + (tc + 1) * 512]
                        if shared:
                            wts = [(MM, mm_a)]
                            p.a(lambda a, hslot=hslot, tc=tc:
                                a.activation(hslot, pA[:, tc * 512:(tc + 1) * 512],
                                             AF.Silu),
                                inc=AC, waits=wts)
                            p.v(lambda v, hslot=hslot, tc=tc:
                                v.tensor_mul(hslot, hslot,
                                             pA2[:, tc * 512:(tc + 1) * 512]),
                                inc=VE, waits=[(AC, AC.n)])
                            slot_ve[tc] = VE.n
                        else:
                            p.v(lambda v, tc=tc:
                                v.tensor_mul(vbuf, pA2[:, tc * 512:(tc + 1) * 512],
                                             zc[:, tc * 512:(tc + 1) * 512]),
                                waits=[(MM, mm_a), (zc_ch, zc_ld), (AC, AC.n)])
                            p.v(lambda v, tc=tc:
                                v.tensor_mul(ubuf, pA[:, tc * 512:(tc + 1) * 512],
                                             zc[:, tc * 512:(tc + 1) * 512]),
                                inc=VE)
                            slot_ve[tc] = VE.n
                            wts = [(VE, VE.n)]
                            if first_h[0]:
                                wts.append((LY, zout_ly))
                                first_h[0] = False
                            p.a(lambda a, hslot=hslot:
                                a.activation(hslot, ubuf, AF.Silu),
                                inc=AC, waits=wts)
                            p.v(lambda v, hslot=hslot:
                                v.tensor_mul(hslot, hslot, vbuf),
                                inc=VE, waits=[(AC, AC.n)])
                    slot_mm[i1s % 3] = MM.n
                    slot_mm[i3s % 3] = MM.n
                zc_ve_prev[0] = VE.n
                h_ve = VE.n
                if e == NE:
                    lnx_free_mm[0] = MM.n
                for do in range(NDO):
                    iw2 = w2idx[0]
                    w2idx[0] += 1
                    ch2 = w2_ch[iw2 % 2]
                    src2 = sw2H[do] if shared else w2H[e, do]
                    dmaW(ch2, w2s(iw2)[:, :], src2,
                         waits=[(MM, slot_mm.get(("w2", iw2 % 2), guard_mm))])
                    n2 = ch2.n
                    for tc in range(ntc):
                        pbank = pB if (do % 2 == 0) else pC
                        for j in range(NJ):
                            wts = []
                            if j == 0:
                                wts = [(ch2, n2),
                                       (VE, max(epi_ve.get((do % 2, tc), 0), h_ve))]
                                if do == 0 and tc == 0:
                                    wts.append((AC, AC.n))
                            p.t(lambda t, do=do, tc=tc, j=j, iw2=iw2, pbank=pbank:
                                t.matmul(pbank[:, tc * 512:(tc + 1) * 512],
                                         w2s(iw2)[:, j * 128:(j + 1) * 128],
                                         hsb[:, j * TOK + tc * 512:
                                             j * TOK + (tc + 1) * 512],
                                         start=(j == 0), stop=(j == NJ - 1)),
                                inc=MM, waits=wts)
                        p.v(lambda v, do=do, tc=tc, pbank=pbank:
                            v.scalar_tensor_tensor(
                                xr(do, tc * 512, (tc + 1) * 512),
                                pbank[:, tc * 512:(tc + 1) * 512], 0.5,
                                xr(do, tc * 512, (tc + 1) * 512),
                                ALU.mult, ALU.add),
                            inc=VE, waits=[(MM, MM.n)])
                        epi_ve[(do % 2, tc)] = VE.n
                        if do == 7:
                            spill_ve8[0] = VE.n
                        if do == NDO - 1:
                            spill_ve16[0] = VE.n
                    slot_mm[("w2", iw2 % 2)] = MM.n
            ffn_dbg[0] = False

        if PH >= -2:
            emit_load_x(x_peer)
        if PH >= 1:
            emit_ln(lambda k: lnx(k))
        if PH >= 2:
            emit_ffn(lambda k: lnx(k))
        if PH >= 3:
            emit_ln(lambda k: l2x(k, 0, TOK))

        if PH >= 4:
            emit_load_x(x_own)
            emit_ln(lambda k: lnx(k))
            emit_ffn(lambda k: lnx(k))
            emit_ln(lambda k: l2x(k, TOK, S))
        if PH >= 0:
            dmaY(xspill[:, 0:8 * TOK], bigA[:, 0:8 * TOK], waits=[(VE, spill_ve8[0])])
            dmaY(xspill[:, 8 * TOK:], bigA[:, 8 * TOK:], waits=[(VE, spill_ve16[0])])
        spill_ly = LY.n

        if 0 <= PH < 5:
            dmaY(bigA[:, :], xspill[:, :], waits=[(LY, spill_ly)])
        _full = PH >= 5
        BA = bigA.ap().bitcast(BF16)
        VOFF = 0
        PTOFF = 8448
        KOFF = 16640
        QOFF = 20736
        O1OFF = 21760
        RTOFF = 22784
        TBOFF = 23808
        QSOFF = 24320
        CCOFF = 27392
        CSOFF = 29440

        def Vslot(st, hh):
            return BA[:, VOFF + st * 528 + hh * 132: VOFF + st * 528 + hh * 132 + 132]

        def Vtile(st):
            return BA[:, VOFF + st * 528: VOFF + st * 528 + 528]

        def Pts(st):
            return BA[:, PTOFF + st * 512: PTOFF + (st + 1) * 512]

        def Kbuf(par):
            return BA[:, KOFF + par * 2048: KOFF + (par + 1) * 2048]

        Qb = BA[:, QOFF: QOFF + 1024]
        o1n = BA[:, O1OFF:O1OFF + 1024].bitcast(F32)
        rtmp = BA[:, RTOFF:RTOFF + 1024].bitcast(F32)
        tmpB = BA[:, TBOFF:TBOFF + 512]
        qsq = BA[:, QSOFF:QSOFF + 3072]
        cC = BA[:, CCOFF:CCOFF + 2048]
        cS2 = BA[:, CSOFF:CSOFF + 2048]

        ln2_ve = VE.n
        ln2_ac = AC.n
        if _full:
            p.v(lambda v: v.memset(BA[:, VOFF:VOFF + 8448], 1.0), inc=VE,
                waits=[(LY, spill_ly), (AC, ln2_ac)])
        vones_ve = VE.n
        if _full:
            dmaW(SL[7], cC[:, :], ropeC[:, :],
                 waits=[(LY, spill_ly), (VE, ln2_ve), (AC, ln2_ac)])
            dmaW(SL[7], cS2[:, :], ropeS2[:, :])
        rope_lw = SL[7].n

        wv_ch = SL[1]
        wk_ch = SL[2]
        wq_ch = SL[3]

        wvslot = wa[:, 0:8192]

        def wkslot(par):
            return hsb[:, par * 2048:(par + 1) * 2048]

        def wqslot(par):
            return hsb[:, 4096 + par * 2048: 4096 + (par + 1) * 2048]

        prevmm = {}
        pt_mm = {}
        pcg = {0: [(VE, vones_ve)], 1: [(VE, vones_ve)]}
        pbg = {0: [], 1: []}
        obg = {0: [], 1: []}
        sag = {0: [], 1: []}
        atb_mm = [0]

        for g4 in range(4 if _full else 0):
            dmaW(wv_ch, wvslot[:, :], wvH[g4],
                 waits=[(MM, prevmm.get("wv", MM.n))])
            nwv = wv_ch.n
            for st in range(16):
                vb = (st % 2) * 512
                for k in range(NK):
                    wts = []
                    if k == 0:
                        wts = [(wv_ch, nwv)] + pcg[st % 2] + (obg[1] if st < 2 else [])
                    p.t(lambda t, st=st, k=k, vb=vb:
                        t.matmul(pC[:, vb:vb + 512],
                                 l2x(k, st * 128, (st + 1) * 128),
                                 wvslot[:, k * 512:(k + 1) * 512],
                                 start=(k == 0), stop=(k == NK - 1)),
                        inc=MM, waits=wts)
                p.v(lambda v, st=st, vb=vb:
                    v.tensor_copy(
                        Vtile(st).rearrange("p (h c) -> p h c", c=132)[:, :, 0:128],
                        pC[:, vb:vb + 512].rearrange("p (h c) -> p h c", c=128)),
                    inc=VE, waits=[(MM, MM.n)])
                pcg[st % 2] = [(VE, VE.n)]
            v_ve = VE.n

            for hh in range(4):
                h = g4 * 4 + hh
                par = h % 2
                dmaW(wk_ch, wkslot(par)[:, :], wkH[h],
                     waits=[(MM, prevmm.get(("k", par), MM.n))])
                nk_ld = wk_ch.n
                dmaW(wq_ch, wqslot(par)[:, :], wqH[h],
                     waits=[(MM, prevmm.get(("q", par), MM.n))])
                nq_ld = wq_ch.n
                specs = [("k", 0), ("k", 1), ("k", 2), ("k", 3), ("q", 0), ("q", 1)]
                for idx, (kind, sc) in enumerate(specs):
                    pslice = pC[:, (idx % 2) * 512:((idx % 2) + 1) * 512]
                    if kind == "k":
                        dst = Kbuf(par)[:, sc * 512:(sc + 1) * 512]
                        abs0 = sc * 512
                        wch, nld = wk_ch, nk_ld

                        def wsl(k, par=par):
                            return wkslot(par)[:, k * 128:(k + 1) * 128]

                        def rhs(k, sc=sc):
                            return l2x(k, sc * 512, (sc + 1) * 512)
                    else:
                        dst = Qb[:, sc * 512:(sc + 1) * 512]
                        abs0 = 1024 + sc * 512
                        wch, nld = wq_ch, nq_ld

                        def wsl(k, par=par):
                            return wqslot(par)[:, k * 128:(k + 1) * 128]

                        def rhs(k, sc=sc):
                            return l2x(k, 1024 + sc * 512, 1024 + (sc + 1) * 512)
                    for k in range(NK):
                        wts = []
                        if k == 0:
                            wts = [(wch, nld)] + pcg[idx % 2]
                            if idx % 2 == 0:
                                wts += obg[1]
                        p.t(lambda t, k=k, pslice=pslice, wsl=wsl, rhs=rhs:
                            t.matmul(pslice, wsl(k), rhs(k),
                                     start=(k == 0), stop=(k == NK - 1)),
                            inc=MM, waits=wts)
                    mm_p = MM.n
                    p.a(lambda a, dst=dst, pslice=pslice:
                        a.copy(dst, pslice), inc=AC, waits=[(MM, mm_p)])
                    ac_raw = AC.n
                    p.t(lambda t, dst=dst, pslice=pslice:
                        t.matmul(pslice, permb[:, :], dst, start=True, stop=True),
                        inc=MM, waits=[(AC, ac_raw)])
                    mm_perm = MM.n
                    p.v(lambda v, dst=dst, abs0=abs0:
                        v.tensor_mul(tmpB[:, 0:512], dst, cC[:, abs0:abs0 + 512]),
                        waits=[(AC, ac_raw), (SL[7], rope_lw), (MM, atb_mm[0])])
                    p.v(lambda v, pslice=pslice, abs0=abs0:
                        v.tensor_mul(rtmp[:, 0:512], pslice, cS2[:, abs0:abs0 + 512]),
                        inc=VE, waits=[(MM, mm_perm)])
                    pcg[idx % 2] = [(VE, VE.n), (AC, ac_raw)]
                    p.v(lambda v, dst=dst:
                        v.tensor_add(dst, tmpB[:, 0:512], rtmp[:, 0:512]),
                        inc=VE)
                rope_ve = VE.n
                p.a(lambda a, par=par: a.activation(qsq[:, 0:2048], Kbuf(par), AF.Square),
                    inc=AC, waits=[(VE, rope_ve)])
                p.a(lambda a: a.activation(qsq[:, 2048:3072], Qb, AF.Square), inc=AC)
                sq_ac = AC.n
                km = sma[0:1, 512:513]
                k2 = sma[0:1, 513:514]
                qm = sma[0:1, 514:515]
                kmm = sma[0:1, 515:516]
                lnk = sma[0:1, 516:517]
                lnq = sma[0:1, 517:518]
                lsum = sma[0:1, 518:519]
                eB = sma[0:1, 519:520]
                nB = sma[0:1, 520:521]
                p.t(lambda t:
                    t.matmul(pB[0:1, 0:512], onesb[:, 0:1],
                             qsq[:, 0:512], start=True, stop=True),
                    inc=MM, waits=[(AC, sq_ac), (GP, GP.n)] + pbg[0] + pbg[1]
                    + obg[0])
                p.t(lambda t:
                    t.matmul(pB[0:1, 512:1024], onesb[:, 0:1],
                             qsq[:, 512:1024], start=True, stop=True),
                    inc=MM)
                p.v(lambda v: v.reduce_max(km, pB[0:1, 0:1024], AX.X),
                    inc=VE, waits=[(MM, MM.n)])
                p.t(lambda t:
                    t.matmul(pB[0:1, 0:512], onesb[:, 0:1],
                             qsq[:, 1024:1536], start=True, stop=True),
                    inc=MM, waits=[(VE, VE.n)])
                p.t(lambda t:
                    t.matmul(pB[0:1, 512:1024], onesb[:, 0:1],
                             qsq[:, 1536:2048], start=True, stop=True),
                    inc=MM)
                p.v(lambda v: v.reduce_max(k2, pB[0:1, 0:1024], AX.X),
                    inc=VE, waits=[(MM, MM.n)])
                p.g(lambda g: g.tensor_scalar_max(kmm, km, k2[0:1, 0:1]),
                    inc=GP, waits=[(VE, VE.n)])
                p.t(lambda t:
                    t.matmul(pB[0:1, 0:512], onesb[:, 0:1],
                             qsq[:, 2048:2560], start=True, stop=True),
                    inc=MM, waits=[(VE, VE.n)])
                p.t(lambda t:
                    t.matmul(pB[0:1, 512:1024], onesb[:, 0:1],
                             qsq[:, 2560:3072], start=True, stop=True),
                    inc=MM)
                p.v(lambda v: v.reduce_max(qm, pB[0:1, 0:1024], AX.X),
                    inc=VE, waits=[(MM, MM.n)])
                p.a(lambda a: a.activation(lnk, kmm, AF.Ln),
                    waits=[(GP, GP.n), (VE, VE.n)])
                p.a(lambda a: a.activation(lnq, qm, AF.Ln), inc=AC)
                p.g(lambda g: g.tensor_scalar_add(lsum, lnk, lnq[0:1, 0:1]),
                    inc=GP, waits=[(AC, AC.n)])
                p.a(lambda a: a.activation(eB, lsum, AF.Exp, scale=0.5,
                                           bias=cbias[0:1, 1:2]),
                    waits=[(GP, GP.n)])
                p.a(lambda a: a.activation(nB, eB, AF.Copy, scale=-1.0), inc=AC)
                p.a(lambda a: a.copy(lhl[0:1, 2:3], nB), inc=AC)
                p.t(lambda t: t.matmul(pB[:, 512:513], onesr[0:1, :],
                                       lhl[0:1, 2:3], start=True, stop=True),
                    inc=MM, waits=[(AC, AC.n), (VE, VE.n)])
                p.a(lambda a: a.copy(nBc[:, 0:1], pB[:, 512:513]),
                    inc=AC, waits=[(MM, MM.n)])
                p.a(lambda a: a.copy(nBc[:, 1:2], pB[:, 512:513]), inc=AC)
                pbg[0] = [(VE, VE.n)]
                pbg[1] = [(AC, AC.n)]

                atb4 = tmpB
                for qc in range(2):
                    for half in range(2):
                        r0 = half * 64
                        exp_acs = [0] * 16
                        for st in range(16):
                            pair = st // 2
                            psl = pA if (pair % 2 == 0) else pA2
                            col = (st % 2) * 512
                            wts = []
                            if st == 0 and half == 0 and qc == 0:
                                wts.append((VE, rope_ve))
                            if st % 2 == 0:
                                wts += sag[pair % 2]
                            p.t(lambda t, st=st, r0=r0, qc=qc, par=par, psl=psl,
                                col=col:
                                t.matmul(psl[:, col:col + 512],
                                         Kbuf(par)[r0:r0 + 64, st * 128:(st + 1) * 128],
                                         Qb[r0:r0 + 64, qc * 512:(qc + 1) * 512],
                                         start=True, stop=True),
                                inc=MM, waits=wts)
                            if st % 2 == 1:
                                mm_s = MM.n
                                wts2 = [(MM, mm_s)]
                                for stq in (st - 1, st):
                                    if ("pt", stq) in pt_mm:
                                        wts2.append((MM, pt_mm[("pt", stq)]))
                                p.a(lambda a, st=st, half=half, psl=psl:
                                    a.activation(Pts(st - 1).tensor.ap()
                                                 [:, PTOFF + (st - 1) * 512:
                                                  PTOFF + (st + 1) * 512]
                                                 if False else
                                                 BA[:, PTOFF + (st - 1) * 512:
                                                    PTOFF + (st + 1) * 512],
                                                 psl[:, 0:1024], AF.Exp,
                                                 bias=nBc[:, half:half + 1],
                                                 scale=0.125),
                                    inc=AC, waits=wts2)
                                exp_acs[st - 1] = AC.n
                                exp_acs[st] = AC.n
                                sag[pair % 2] = [(AC, AC.n)]
                        for qt in range(4):
                            ob = pB if (qt % 2 == 0) else pC
                            for st in range(16):
                                wts = []
                                if st == 0:
                                    wts += obg[qt % 2]
                                    wts += (pbg[0] if qt % 2 == 0
                                            else pcg[0] + pcg[1])
                                if qt == 0:
                                    wts.append((AC, exp_acs[st]))
                                p.t(lambda t, st=st, qt=qt, hh=hh, ob=ob:
                                    t.matmul(ob[:, 0:129],
                                             Pts(st)[:, qt * 128:(qt + 1) * 128],
                                             Vslot(st, hh)[:, 0:129],
                                             start=(st == 0), stop=(st == 15)),
                                    inc=MM, waits=wts)
                                pt_mm[("pt", st)] = MM.n
                            mm_pv = MM.n
                            if half == 0:
                                p.v(lambda v, ob=ob: v.reciprocal(
                                    zq[:, 0:1], ob[:, 128:129]),
                                    inc=VE, waits=[(MM, mm_pv)])
                                p.wait("v", VE)
                                p.v(lambda v, qt=qt, ob=ob: v.tensor_single_scalar(
                                    o1n[:, qt * 128:(qt + 1) * 128],
                                    ob[:, 0:128], zq[:, 0:1], ALU.mult), inc=VE)
                                obg[qt % 2] = [(VE, VE.n)]
                            else:
                                at0 = att[:, 0:128]
                                at1 = att[:, 128:256]
                                p.v(lambda v, ob=ob: v.reciprocal(
                                    zq[:, 1:2], ob[:, 128:129]),
                                    inc=VE, waits=[(MM, mm_pv)])
                                p.wait("v", VE)
                                p.v(lambda v: v.tensor_mul(zq[:, 2:3], zq[:, 1:2],
                                                           lamc[:, :]), inc=VE)
                                p.wait("v", VE)
                                p.v(lambda v, qt=qt, ob=ob, at0=at0:
                                    v.scalar_tensor_tensor(
                                        at0, ob[:, 0:128], zq[:, 2:3],
                                        o1n[:, qt * 128:(qt + 1) * 128],
                                        ALU.mult, ALU.add), inc=VE)
                                obg[qt % 2] = [(VE, VE.n)]
                                p.v(lambda v, at0=at0, at1=at1:
                                    v.tensor_mul(at1, at0, at0))
                                p.v(lambda v, at1=at1:
                                    v.reduce_sum(zq[:, 3:4], at1, AX.X), inc=VE)
                                p.a(lambda a: a.activation(zq[:, 3:4], zq[:, 3:4],
                                                           AF.Ln, scale=1.0 / 128,
                                                           bias=cbias[:, 0:1]),
                                    waits=[(VE, VE.n)])
                                p.a(lambda a: a.activation(zq[:, 3:4], zq[:, 3:4],
                                                           AF.Exp, scale=-0.5,
                                                           bias=cbias[:, 2:3]), inc=AC)
                                p.v(lambda v, at0=at0: v.tensor_single_scalar(
                                    at0, at0, zq[:, 3:4], ALU.mult),
                                    waits=[(AC, AC.n)])
                                w = [(MM, atb_mm[0])] if qt == 0 else []
                                p.v(lambda v, at0=at0, qt=qt:
                                    v.tensor_mul(atb4[:, qt * 128:(qt + 1) * 128],
                                                 at0, gamb[:, :]),
                                    inc=VE, waits=w)
                        if half == 1:
                            pass
                    att_ve2 = VE.n
                    cp_ve = {}
                    for qt in range(4):
                        tp = pA if qt % 2 == 0 else pA2
                        wts = [(VE, att_ve2), (AC, AC.n)] if qt == 0 else []
                        if qt >= 2:
                            wts = [(VE, cp_ve[qt - 2])]
                        p.t(lambda t, qt=qt, tp=tp:
                            t.transpose(tp[:, 0:64].bitcast(BF16)[:, 0:128],
                                        atb4[:, qt * 128:(qt + 1) * 128],
                                        identb[:, :]),
                            inc=MM, waits=wts)
                        p.v(lambda v, h=h, qc=qc, qt=qt, tp=tp:
                            v.tensor_copy(
                                mid[:, h * 1024 + qc * 512 + qt * 128:
                                    h * 1024 + qc * 512 + (qt + 1) * 128],
                                tp[:, 0:64].bitcast(BF16)[:, 0:128]),
                            inc=VE, waits=[(MM, MM.n)])
                        cp_ve[qt] = VE.n
                    atb_mm[0] = MM.n
                    sag[0] = sag[0] + [(VE, VE.n)]
                    sag[1] = sag[1] + [(VE, VE.n)]
                prevmm[("k", par)] = MM.n
                prevmm[("q", par)] = MM.n
            prevmm["wv"] = MM.n
        attn_ve = VE.n
        if os.environ.get("ATH_DBG") == "attn":
            dmaW(SL[0], dbg2.bitcast(BF16)[:, 0:NK * TOK], mid[:, :],
                 waits=[(VE, attn_ve)])
            dmaY(dbgd[299:300, 0:1], lamc[0:1, 0:1], waits=[(VE, attn_ve)])
            dmaY(dbgd[299:300, 1:2], lamc[64:65, 0:1], waits=[(VE, attn_ve)])
            dmaY(dbgd[298:299, 0:8], sma[0:1, 384:392], waits=[(VE, attn_ve)])
            dmaY(dbgd[297:298, 0:64], sma[0:1, 0:64], waits=[(VE, attn_ve)])
            dmaY(dbgd[296:297, 0:64], sma[0:1, 256:320], waits=[(VE, attn_ve)])

        if _full:
            dmaY(bigA[:, :], xspill[:, :],
                 waits=[(VE, attn_ve), (MM, MM.n), (AC, AC.n), (GP, GP.n)])
        reload_ly = LY.n
        wo_ch = SL[8]
        wo_prev = {}
        for fo in range(NDO if PH >= 6 else 0):
            sl = fo % 2

            def wos(sl=sl):
                return wa[:, sl * 2048:(sl + 1) * 2048]
            dmaW(wo_ch, wos()[:, :], woH[fo],
                 waits=[(MM, wo_prev.get(sl, MM.n))])
            nwo = wo_ch.n
            for qc in range(2):
                pbank = pB if (qc % 2 == 0) else pC
                for hh2 in range(NH):
                    wts = []
                    if hh2 == 0:
                        wts = [(wo_ch, nwo), (VE, VE.n)]
                    p.t(lambda t, qc=qc, hh2=hh2, pbank=pbank, wos=wos:
                        t.matmul(pbank[:, 0:512],
                                 wos()[:, hh2 * 128:(hh2 + 1) * 128],
                                 mid[:, hh2 * 1024 + qc * 512:
                                     hh2 * 1024 + (qc + 1) * 512],
                                 start=(hh2 == 0), stop=(hh2 == NH - 1)),
                        inc=MM, waits=wts)
                p.v(lambda v, fo=fo, qc=qc, pbank=pbank:
                    v.tensor_add(xr(fo, qc * 512, (qc + 1) * 512),
                                 pbank[:, 0:512],
                                 xr(fo, qc * 512, (qc + 1) * 512)),
                    inc=VE, waits=[(MM, MM.n), (LY, reload_ly)])
            wo_prev[sl] = MM.n

        if PH >= 7:
            emit_ln(lambda k: lnx(k))
            emit_ffn(lambda k: lnx(k))

        midf2 = mid.ap().bitcast(F32)

        def stage(par):
            return midf2[:, par * 2048:(par + 1) * 2048]
        st_ch = SL[9]
        mm_out0 = MM.n
        ve_out0 = VE.n
        cp_cnt = {}
        tslots = [pC[:, 0:128], pC[:, 512:640], pB[:, 0:128], pB[:, 512:640]]
        for tt in range(8):
            par = tt % 2
            for k in range(NK):
                i = tt * NK + k
                pk = i % 4
                if i < 4:
                    wts = [(VE, ve_out0)]
                    if i == 0:
                        wts.append((MM, mm_out0))
                        wts.append((LY, LY.n))
                else:
                    wts = [(VE, cp_cnt[i - 4])]
                if tt >= 2 and k == 0:
                    wts.append((st_ch, st_ch.n - 16))
                p.t(lambda t, tt=tt, k=k, pk=pk:
                    t.transpose(tslots[pk],
                                xr(k, tt * 128, (tt + 1) * 128), identf[:, :]),
                    inc=MM, waits=wts)
                p.v(lambda v, tt=tt, k=k, pk=pk, par=par:
                    v.tensor_copy(stage(par)[:, k * 128:(k + 1) * 128],
                                  tslots[pk]),
                    inc=VE, waits=[(MM, MM.n)])
                cp_cnt[i] = VE.n
            dmaY(out_d[tt * 128:(tt + 1) * 128, :], stage(par)[:, :],
                 waits=[(VE, VE.n)], ch=st_ch)
        p.wait("y", st_ch)
        p.wait("g", st_ch)

        p.check()

        with nc.Block() as block:
            block.tensor(lambda te: ([f(te) for f in p.q["t"]], None)[1])
            block.vector(lambda ve: ([f(ve) for f in p.q["v"]], None)[1])
            block.scalar(lambda ac: ([f(ac) for f in p.q["a"]], None)[1])
            block.gpsimd(lambda gp: ([f(gp) for f in p.q["g"]], None)[1])
            block.sync(lambda sy: ([f(sy) for f in p.q["y"]], None)[1])

    return nc


def _host_layouts(inputs):
    Wq = np.asarray(inputs["L0_attn_Wq"])
    Wk = np.asarray(inputs["L0_attn_Wk"])
    Wv = np.asarray(inputs["L0_attn_Wv"])
    Wo = np.asarray(inputs["L0_attn_Wo"])
    w1 = np.asarray(inputs["L0_ffn_w1"])
    w2 = np.asarray(inputs["L0_ffn_w2"])
    w3 = np.asarray(inputs["L0_ffn_w3"])
    sw1 = np.asarray(inputs["L0_ffn_sw1"])
    sw2 = np.asarray(inputs["L0_ffn_sw2"])
    sw3 = np.asarray(inputs["L0_ffn_sw3"])
    gw = np.asarray(inputs["L0_ffn_gate_w"])
    C = np.ascontiguousarray
    d = {}
    d["wkH"] = C(Wk.reshape(16, 128, 16, 128).transpose(0, 3, 2, 1).reshape(16, 128, 2048))
    d["wqH"] = C(Wq.reshape(16, 128, 16, 128).transpose(0, 3, 2, 1).reshape(16, 128, 2048))
    d["wvH"] = C(Wv.reshape(4, 512, 16, 128).transpose(0, 3, 2, 1).reshape(4, 128, 8192))
    d["woH"] = C(Wo.reshape(16, 128, 16, 128).transpose(0, 3, 2, 1).reshape(16, 128, 2048))
    d["gateH"] = C(gw.T.reshape(16, 128, 8).transpose(1, 0, 2).reshape(128, 128))
    d["w1H"] = C(w1.reshape(8, 8, 128, 16, 128).transpose(0, 1, 4, 3, 2)
                 .reshape(8, 8, 128, 2048))
    d["w3H"] = C(w3.reshape(8, 8, 128, 16, 128).transpose(0, 1, 4, 3, 2)
                 .reshape(8, 8, 128, 2048))
    d["w2H"] = C(w2.reshape(8, 16, 128, 8, 128).transpose(0, 1, 4, 3, 2)
                 .reshape(8, 16, 128, 1024))
    d["sw1H"] = C(sw1.reshape(8, 128, 16, 128).transpose(0, 3, 2, 1).reshape(8, 128, 2048))
    d["sw3H"] = C(sw3.reshape(8, 128, 16, 128).transpose(0, 3, 2, 1).reshape(8, 128, 2048))
    d["sw2H"] = C(sw2.reshape(16, 128, 8, 128).transpose(0, 3, 2, 1).reshape(16, 128, 1024))
    d["gamma"] = C(np.asarray(inputs["L0_attn_gamma"]))
    d["ebias"] = C(np.asarray(inputs["L0_ffn_expert_bias"]))
    d["lq1"] = C(np.asarray(inputs["L0_attn_lambda_q1"]))
    d["lk1"] = C(np.asarray(inputs["L0_attn_lambda_k1"]))
    d["lq2"] = C(np.asarray(inputs["L0_attn_lambda_q2"]))
    d["lk2"] = C(np.asarray(inputs["L0_attn_lambda_k2"]))
    d["identf"] = np.eye(128, dtype=np.float32)
    d["identb"] = np.eye(128, dtype=BF)
    permv = np.zeros((128, 128), dtype=BF)
    for i in range(128):
        permv[i ^ 1, i] = 1.0
    d["permb"] = permv
    d["iota1000"] = np.broadcast_to((1000.0 + np.arange(8)).astype(np.float32),
                                    (128, 8)).copy()
    return d


def _rope_tables(positions):
    HDh = 32
    angle = 1.0 / 10000 ** np.linspace(0.0, 1.0, HDh, dtype=np.float32)
    freqs = positions[None, :].astype(np.float32) * angle[:, None]
    cos = np.cos(freqs).astype(np.float32)
    sin = np.sin(freqs).astype(np.float32)
    Cp = np.empty((128, 2048), np.float32)
    S2 = np.empty((128, 2048), np.float32)
    for pp in range(128):
        i = (pp % 64) // 2
        Cp[pp] = cos[i]
        S2[pp] = -sin[i] if (pp % 2 == 0) else sin[i]
    return Cp, S2


def kernel(**inputs):
    x = np.asarray(inputs["L0_attn_x"])
    shared = _host_layouts(inputs)
    nc = build_program()
    in_maps = []
    for c in range(8):
        b, hf = c // 2, c % 2
        xo = np.ascontiguousarray(x[b, hf * 1024:(hf + 1) * 1024])
        xp = np.ascontiguousarray(x[b, (1 - hf) * 1024:(2 - hf) * 1024])
        peer_pos = np.arange((1 - hf) * 1024, (2 - hf) * 1024)
        own_pos = np.arange(hf * 1024, (hf + 1) * 1024)
        Cp, S2 = _rope_tables(np.concatenate([peer_pos, own_pos]))
        m = dict(shared)
        m["x_own"] = xo
        m["x_peer"] = xp
        m["ropeC"] = Cp
        m["ropeS2"] = S2
        in_maps.append(m)
    res = run_bass_kernel_spmd(nc, in_maps, core_ids=list(range(8)))
    out = np.empty((4, 2048, 2048), np.float32)
    for c, r in enumerate(res.results):
        b, hf = c // 2, c % 2
        out[b, hf * 1024:(hf + 1) * 1024] = r["out"]
    return out
